# version 04
# baseline (speedup 1.0000x reference)
import numpy as np
import concourse.bass as bass
import concourse.mybir as mybir
import concourse.library_overlay as library_overlay
from concourse.bass_utils import run_bass_kernel_spmd
from concourse.library_config import mlp as mlp_lib
from contextlib import ExitStack

F32 = mybir.dt.float32
BF16 = mybir.dt.bfloat16
I16 = mybir.dt.int16
AF = mybir.ActivationFunctionType
ALU = mybir.AluOpType
AX = mybir.AxisListType

T = 8192
D = 1024
NT = T // 128
NG = T // 512
NE = 8
CAP = 1024
HEADS = 16
EPS = 1e-5
TRASH = 8192


class Sem:
    def __init__(self, h):
        self.h = h
        self.val = 0

    def bump(self, n):
        self.val += n
        return self.val


class Prog:
    def __init__(self):
        self.items = []
        self._last_wait = {}
        self.marks = {}

    def mark(self, name):
        self.marks[name] = len(self.items)

    def op(self, eng, fn, waits=(), inc=None, inc_n=1):
        ws = []
        for (s, v) in waits:
            if v <= 0:
                continue
            k = (eng, id(s))
            if self._last_wait.get(k, -1) >= v:
                continue
            self._last_wait[k] = v
            ws.append((s, v))
        if inc is not None:
            inc.bump(inc_n)
        self.items.append((eng, fn, ws, inc, inc_n))

    def emit(self, eng_name, eng, max_items=None):
        items = self.items if max_items is None else self.items[:max_items]
        for (e, fn, ws, inc, inc_n) in items:
            if e != eng_name:
                continue
            for (s, v) in ws:
                eng.wait_ge(s.h, v)
            r = fn(eng)
            if inc is not None:
                if isinstance(r, list):
                    r = r[-1]
                assert r is not None, "op with inc returned no instruction"
                r.then_inc(inc.h, inc_n)


def build_nc(debug=False, max_items=None):
    nc = bass.Bass("TRN2", target_bir_lowering=False)

    x = nc.dram_tensor("x", [T, D], F32, kind="ExternalInput").ap()
    w_qkv = nc.dram_tensor("w_qkv", [3 * D, D], F32, kind="ExternalInput").ap()
    b_qkv = nc.dram_tensor("b_qkv", [1, 3 * D], F32, kind="ExternalInput").ap()
    w_g1 = nc.dram_tensor("w_g1", [64, D], F32, kind="ExternalInput").ap()
    b_g1 = nc.dram_tensor("b_g1", [1, 64], F32, kind="ExternalInput").ap()
    w_g2 = nc.dram_tensor("w_g2", [D, 64], F32, kind="ExternalInput").ap()
    b_g2 = nc.dram_tensor("b_g2", [1, D], F32, kind="ExternalInput").ap()
    w_out = nc.dram_tensor("w_out", [D, D], F32, kind="ExternalInput").ap()
    b_out = nc.dram_tensor("b_out", [1, D], F32, kind="ExternalInput").ap()
    ln_g = nc.dram_tensor("ln_g", [1, D], F32, kind="ExternalInput").ap()
    ln_b = nc.dram_tensor("ln_b", [1, D], F32, kind="ExternalInput").ap()
    wg = nc.dram_tensor("wg", [D, NE], F32, kind="ExternalInput").ap()
    w1 = nc.dram_tensor("w1", [NE, D, 4 * D], F32, kind="ExternalInput").ap()
    w2 = nc.dram_tensor("w2", [NE, 4 * D, D], F32, kind="ExternalInput").ap()

    y = nc.dram_tensor("y", [T, D], F32, kind="ExternalOutput").ap()

    x_bf = nc.dram_tensor("x_bf", [T, D], BF16, kind="Internal").ap()
    wqkv_bf = nc.dram_tensor("wqkv_bf", [3 * D, D], BF16, kind="Internal").ap()
    wout_bf = nc.dram_tensor("wout_bf", [D, D], BF16, kind="Internal").ap()
    y1_dram = nc.dram_tensor("y1_dram", [T, D], BF16, kind="Internal").ap()
    ei_dram = nc.dram_tensor("ei_dram", [T + 128, D], BF16, kind="Internal").ap()
    eo_dram = nc.dram_tensor("eo_dram", [T + 128, D], BF16, kind="Internal").ap()

    dbg = {}
    if debug:
        for name, shape, dt in [
            ("d_kvh", [128, 1024], F32), ("d_sumd", [128, 8], F32),
            ("d_lg", [128, 8], F32), ("d_slot", [128, NT], F32),
            ("d_kg", [128, NT], F32), ("d_attn", [128, D], F32),
            ("d_gate", [128, D], F32), ("d_idx", [128, 512], I16),
            ("d_q", [128, 8, 512], F32),
            ("d_expk0", [128, D], F32), ("d_expk63", [128, D], F32),
            ("d_v0", [128, D], F32), ("d_kvraw", [128, 512], F32),
            ("d_expk8", [128, D], F32), ("d_y1r", [128, D], F32),
            ("d_ycm", [128, 1024], F32), ("d_wgr", [128, 64], F32),
            ("d_xt", [128, D], F32), ("d_res", [128, D], F32),
            ("d_lnt2", [128, D], F32), ("d_gcm", [128, 1024], F32),
            ("d_lnbc", [128, D], F32), ("d_lnt", [128, D], F32),
        ]:
            dbg[name] = nc.dram_tensor(name, shape, dt, kind="ExternalOutput").ap()

    st = ExitStack()
    e = st.enter_context

    R1 = e(nc.sbuf_tensor("R1", [128, 32768], BF16))
    R2 = e(nc.sbuf_tensor("R2", [128, 32768], BF16))
    SB = e(nc.sbuf_tensor("SB", [128, 20480], BF16))
    SF = e(nc.sbuf_tensor("SF", [128, 4096], F32))

    bqkv_bf = e(nc.sbuf_tensor("bqkv_bf", [1, 3 * D], BF16))
    bg1_bf = e(nc.sbuf_tensor("bg1_bf", [1, 64], BF16))
    bg2_bf = e(nc.sbuf_tensor("bg2_bf", [1, D], BF16))
    bout_bf = e(nc.sbuf_tensor("bout_bf", [1, D], BF16))
    ones_bfr = e(nc.sbuf_tensor("ones_bfr", [1, 512], BF16))
    ones_f32r = e(nc.sbuf_tensor("ones_f32r", [1, 128], F32))
    ones_colb = e(nc.sbuf_tensor("ones_colb", [128, 1], BF16))
    ident_bf = e(nc.sbuf_tensor("ident_bf", [128, 128], BF16))
    ut_bf = e(nc.sbuf_tensor("ut_bf", [128, 128], BF16))
    iota8 = e(nc.sbuf_tensor("iota8", [128, 8], BF16))
    wgr = e(nc.sbuf_tensor("wgr", [128, 8 * NE], BF16))
    wg1T = e(nc.sbuf_tensor("wg1T", [128, 8 * 64], BF16))
    wg2T = e(nc.sbuf_tensor("wg2T", [64, D], BF16))

    mask_b = e(nc.sbuf_tensor("mask_b", [128, NT * 8], BF16))
    gcol = e(nc.sbuf_tensor("gcol", [128, NT], F32))
    ecol = e(nc.sbuf_tensor("ecol", [128, NT], F32))
    kg128 = e(nc.sbuf_tensor("kg128", [128, NT], F32))
    slot128 = e(nc.sbuf_tensor("slot128", [128, NT], I16))
    idx16 = e(nc.sbuf_tensor("idx16", [128, 512], I16))
    tots = e(nc.sbuf_tensor("tots", [8, NT], F32))
    offsx = e(nc.sbuf_tensor("offsx", [8, NT], F32))
    offs_flat = e(nc.sbuf_tensor("offs_flat", [1, 512], F32))
    t8a = e(nc.sbuf_tensor("t8a", [128, 8], F32))
    t8b = e(nc.sbuf_tensor("t8b", [128, 8], F32))
    t8c = e(nc.sbuf_tensor("t8c", [128, 8], F32))
    t8c_b = e(nc.sbuf_tensor("t8c_b", [128, 8], BF16))
    c1a = e(nc.sbuf_tensor("c1a", [128, 1], F32))
    c1b = e(nc.sbuf_tensor("c1b", [128, 1], F32))
    c1c = e(nc.sbuf_tensor("c1c", [128, 1], F32))
    c1d = e(nc.sbuf_tensor("c1d", [128, 1], F32))
    bnbuf = e(nc.sbuf_tensor("bnbuf", [128, 12], F32))
    mv2 = e(nc.sbuf_tensor("mv2", [128, 2], F32))

    BK = [e(nc.psum_tensor(f"bk{i}", [128, 512], F32)) for i in range(7)]
    PTR = e(nc.psum_tensor("ptr", [128, 1024], BF16))

    def bfview(region, off, shape):
        n = 1
        for s_ in shape[1:]:
            n *= s_
        apv = region[:, off:off + n]
        if len(shape) == 3:
            return apv.rearrange("p (a b) -> p a b", a=shape[1])
        return apv

    xcmA = bfview(R1, 0, [128, 8, 4096])
    xcmB = bfview(R2, 0, [128, 8, 4096])
    wkvT = bfview(SB, 0, [128, 8, 2048])
    expk = SB[:, 16384:17408]
    v_t = SB[:, 17408:18432]
    g2nat = bfview(SB, 18432, [128, 8, 64])
    g1nat = SB[0:64, 18944:19968]
    lng_row = SB[0:1, 18432:19456]
    lnb_row = SB[0:1, 19456:20480]

    wqT = bfview(R2, 0, [128, 8, 1024])
    woutT = bfview(R2, 8192, [128, 8, 1024])
    q_cm = bfview(R2, 16384, [128, 8, 512])
    gh_cm = R2[0:64, 20480:20992]
    ln_t = R2[:, 20992:22016]
    ln_t2 = R2[:, 22016:23040]
    gate_t = R2[:, 23040:24064]
    g_t = R2[:, 24064:25088]
    g_cm = bfview(R2, 25088, [128, 8, 128])
    y1_t = R2[:, 26112:27136]
    lng_bc = R2[:, 27136:28160]
    lnb_bc = R2[:, 28160:29184]
    kvh = R2[:, 29184:30208]

    ycmA = bfview(R1, 0, [128, 8, 4096])
    ycmB = bfview(R2, 0, [128, 8, 4096])
    y1tmA = bfview(R1, 0, [128, 32, 1024])
    y1tmB = bfview(R2, 0, [128, 32, 1024])
    w1sb = bfview(R1, 0, [128, 8, 4096])
    w2sb = bfview(R2, 0, [128, 32, 1024])
    ei_cm = bfview(SB, 0, [128, 8, 1024])
    h_ring = bfview(SB, 8192, [128, 4, 256])
    eo_sb = bfview(SB, 9216, [128, 4, 1024])
    of_sb = bfview(SB, 0, [128, 8, 1024])
    y1c = bfview(SB, 8192, [128, 8, 1024])
    zsrc = SB[:, 16384:18432]

    x_t = [SF[:, 0:1024], SF[:, 1024:2048]]
    iotajp = SF[:, 0:128]
    attn2 = [SF[:, 2048:3072], SF[:, 3072:4096]]
    res_f = attn2[0]
    outstage = [SF[:, 0:1024], SF[:, 1024:2048]]

    sems = {}
    for name in ["gld", "yld", "tr", "scst", "tp", "vp", "sp", "gsc"]:
        sems[name] = Sem(e(nc.semaphore(name)))
    gld, yld, tr, scst, tp, vp, sp, gsc = (sems[k] for k in
        ["gld", "yld", "tr", "scst", "tp", "vp", "sp", "gsc"])

    P = Prog()

    P.op("gpsimd", lambda g: g.iota(iota8[:], [[1, 8]], base=0, channel_multiplier=0,
                                    allow_small_or_imprecise_dtypes=True), inc=gld)
    P.op("gpsimd", lambda g: g.iota(iotajp[:], [[1, 128]], base=0, channel_multiplier=-1,
                                    allow_small_or_imprecise_dtypes=True), inc=gld)
    g_iota = gld.val
    P.op("gpsimd", lambda g: g.load_library(mlp_lib))
    P.op("vector", lambda v: v.tensor_single_scalar(ident_bf[:], iotajp[:], 0.0, ALU.is_equal),
         waits=[(gld, g_iota)], inc=vp)
    P.op("vector", lambda v: v.tensor_single_scalar(ut_bf[:], iotajp[:], 0.0, ALU.is_ge), inc=vp)
    P.op("vector", lambda v: v.memset(ones_bfr[:], 1.0), inc=vp)
    P.op("vector", lambda v: v.memset(ones_f32r[:], 1.0), inc=vp)
    P.op("vector", lambda v: v.memset(ones_colb[:], 1.0), inc=vp)
    P.op("vector", lambda v: v.memset(zsrc[:], 0.0), inc=vp)
    P.op("vector", lambda v: v.memset(idx16[:], 0), inc=vp)
    v_zero = vp.val

    P.op("gpsimd", lambda g: g.dma_start(out=wqkv_bf[1024:3072, :], in_=w_qkv[1024:3072, :]),
         inc=gld, inc_n=16)
    g_wkvc = gld.val
    P.op("gpsimd", lambda g: g.dma_start(out=x_bf[0:4096, :], in_=x[0:4096, :]), inc=gld, inc_n=16)
    g_xa = gld.val
    P.op("gpsimd", lambda g: g.dma_start(out=x_bf[4096:8192, :], in_=x[4096:8192, :]), inc=gld, inc_n=16)
    g_xb = gld.val
    P.op("gpsimd", lambda g: g.dma_start(out=wqkv_bf[0:1024, :], in_=w_qkv[0:1024, :]),
         inc=gld, inc_n=16)
    g_wqkv = gld.val
    P.op("gpsimd", lambda g: g.dma_start(out=wout_bf, in_=w_out), inc=gld, inc_n=16)
    g_wout = gld.val
    P.op("gpsimd", lambda g: g.dma_start(out=bqkv_bf[:], in_=b_qkv), inc=gld, inc_n=16)
    P.op("gpsimd", lambda g: g.dma_start(out=bg1_bf[:], in_=b_g1), inc=gld, inc_n=16)
    P.op("gpsimd", lambda g: g.dma_start(out=bg2_bf[:], in_=b_g2), inc=gld, inc_n=16)
    P.op("gpsimd", lambda g: g.dma_start(out=bout_bf[:], in_=b_out), inc=gld, inc_n=16)
    P.op("gpsimd", lambda g: g.dma_start(
        out=wgr[:].rearrange("p (c e) -> p c e", e=NE),
        in_=wg.rearrange("(c p) e -> p c e", p=128),
        allow_slow_non_contiguous=True), inc=gld, inc_n=16)
    P.op("gpsimd", lambda g: g.dma_start(out=g1nat, in_=w_g1), inc=gld, inc_n=16)
    P.op("gpsimd", lambda g: g.dma_start(
        out=g2nat[:], in_=w_g2.rearrange("(c p) d -> p c d", p=128)), inc=gld, inc_n=16)
    g_smallw = gld.val

    def mk_g1tr(tensor):
        r = None
        for c in range(8):
            r = tensor.transpose(PTR[:, c * 64:(c + 1) * 64],
                                 g1nat[:, c * 128:(c + 1) * 128], ident_bf[0:64, 0:64])
        return r
    P.op("tensor", mk_g1tr, waits=[(gld, g_smallw), (vp, v_zero)], inc=tp)
    P.op("vector", lambda v: v.tensor_copy(wg1T[:], PTR[:, 0:512]),
         waits=[(tp, tp.val)], inc=vp)
    v_g1 = vp.val
    def mk_g2tr(tensor):
        r = None
        for c in range(8):
            r = tensor.transpose(PTR[0:64, c * 128:(c + 1) * 128],
                                 g2nat[:, c, :], ident_bf[:])
        return r
    P.op("tensor", mk_g2tr, waits=[(vp, v_g1)], inc=tp)
    P.op("vector", lambda v: v.tensor_copy(wg2T[:], PTR[0:64, :]),
         waits=[(tp, tp.val)], inc=vp)
    v_pro0 = vp.val
    t_protr = tp.val
    P.op("gpsimd", lambda g: g.dma_start(out=lng_row, in_=ln_g),
         waits=[(tp, t_protr)], inc=gld, inc_n=16)
    P.op("gpsimd", lambda g: g.dma_start(out=lnb_row, in_=ln_b), inc=gld, inc_n=16)
    g_lnrows = gld.val

    for ch in range(32):
        P.op("gpsimd", lambda g, ch=ch: g.dma_start(
            out=ei_dram[ch * 256:(ch + 1) * 256, :].rearrange("(a p) d -> p a d", p=128),
            in_=zsrc[:].rearrange("p (a d) -> p a d", d=1024)),
            waits=[(vp, v_zero)], inc=gld, inc_n=16)
    P.op("gpsimd", lambda g: g.dma_start(
        out=ei_dram[8192:8320, :], in_=zsrc[:, 0:1024]), inc=gld, inc_n=16)
    P.op("gpsimd", lambda g: g.dma_start(
        out=eo_dram[8192:8320, :], in_=zsrc[:, 0:1024]), inc=gld, inc_n=16)
    g_zero = gld.val

    y_xcmh = {}
    for half, (dst, gw) in enumerate([(xcmA, g_xa), (xcmB, g_xb)]):
        for c in range(8):
            P.op("sync", lambda s, dst=dst, c=c, half=half: s.dma_start(
                out=dst[:, c, :],
                in_=x_bf[half * 4096:(half + 1) * 4096, c * 128:(c + 1) * 128],
                transpose=True), waits=[(gld, gw)], inc=tr, inc_n=16)
        y_xcmh[half] = tr.val
    y_xcm = tr.val
    for kv_i in range(2):
        for c in range(8):
            P.op("sync", lambda s, kv_i=kv_i, c=c: s.dma_start(
                out=wkvT[:, c, kv_i * 1024:(kv_i + 1) * 1024],
                in_=wqkv_bf[(1 + kv_i) * 1024:(2 + kv_i) * 1024, c * 128:(c + 1) * 128],
                transpose=True), waits=[(gld, g_wkvc)], inc=tr, inc_n=16)
    y_wkv = tr.val

    P.mark("p1")
    kvp = BK[6]
    sumdp = BK[4]
    v_p1base = vp.val
    s_p1base = sp.val
    for t in range(NT):
        half = xcmA if t < 32 else xcmB
        toff = (t % 32) * 128
        tw = [(tr, y_xcmh[0] if t < 32 else y_xcmh[1]), (tr, y_wkv),
              (gld, g_smallw), (vp, v_p1base)]
        if t > 0:
            tw += [(sp, s_p1base + 2 * t), (vp, v_p1base + 2 * t)]
        for hh in range(2):
            def mk_k(tensor, half=half, toff=toff, hh=hh):
                for c in range(8):
                    tensor.matmul(BK[hh][:], half[:, c, toff:toff + 128],
                                  wkvT[:, c, hh * 512:(hh + 1) * 512],
                                  start=(c == 0), stop=False)
                return tensor.matmul(BK[hh][:], ones_bfr[0:1, 0:128],
                                     bqkv_bf[0:1, 1024 + hh * 512:1024 + (hh + 1) * 512],
                                     start=False, stop=True)
            P.op("tensor", mk_k, waits=tw if hh == 0 else (), inc=tp)
        for hh in range(2):
            def mk_v(tensor, half=half, toff=toff, hh=hh):
                for c in range(8):
                    tensor.matmul(BK[2 + hh][:], half[:, c, toff:toff + 128],
                                  wkvT[:, c, 1024 + hh * 512:1024 + (hh + 1) * 512],
                                  start=(c == 0), stop=False)
                return tensor.matmul(BK[2 + hh][:], ones_bfr[0:1, 0:128],
                                     bqkv_bf[0:1, 2048 + hh * 512:2048 + (hh + 1) * 512],
                                     start=False, stop=True)
            P.op("tensor", mk_v, inc=tp)
        base = tp.val - 4
        for hh in range(2):
            P.op("scalar", lambda s, hh=hh: s.activation(
                expk[:, hh * 512:(hh + 1) * 512], BK[hh][:], AF.Exp),
                waits=[(tp, base + 1 + hh)], inc=sp)
        for hh in range(2):
            P.op("vector", lambda v, hh=hh: v.tensor_copy(
                v_t[:, hh * 512:(hh + 1) * 512], BK[2 + hh][:]),
                waits=[(tp, base + 3 + hh)], inc=vp)
        def mk_kv(tensor, t=t):
            r = None
            for h in range(HEADS):
                pb = (h % 2) * 64
                fb = (h // 2) * 64
                r = tensor.matmul(kvp[pb:pb + 64, fb:fb + 64],
                                  expk[:, h * 64:(h + 1) * 64],
                                  v_t[:, h * 64:(h + 1) * 64],
                                  start=(t == 0), stop=(t == NT - 1))
            for c in range(8):
                r = tensor.matmul(sumdp[:, c:c + 1],
                                  expk[:, c * 128:(c + 1) * 128],
                                  ones_colb[:],
                                  start=(t == 0), stop=(t == NT - 1))
            return r
        if debug and t == 0:
            P.op("gpsimd", lambda g: g.dma_start(out=dbg["d_expk0"], in_=expk[:]),
                 waits=[(sp, sp.val)], inc=gld, inc_n=16)
            P.op("gpsimd", lambda g: g.dma_start(out=dbg["d_v0"], in_=v_t[:]),
                 waits=[(vp, vp.val)], inc=gld, inc_n=16)
        kvw = [(sp, sp.val), (vp, vp.val)]
        if debug and t == 1:
            kvw.append((gld, gld.val))
        if debug and t == NT - 1:
            P.op("gpsimd", lambda g: g.dma_start(out=dbg["d_expk63"], in_=expk[:]),
                 waits=[(sp, sp.val)], inc=gld, inc_n=16)
        if debug and t == 8:
            P.op("gpsimd", lambda g: g.dma_start(out=dbg["d_expk8"], in_=expk[:]),
                 waits=[(sp, sp.val)], inc=gld, inc_n=16)
            kvw.append((gld, gld.val))
        P.op("tensor", mk_kv, waits=kvw, inc=tp)
    t_p1 = tp.val

    P.op("vector", lambda v: v.tensor_copy(t8a[:], sumdp[:, 0:8]), waits=[(tp, t_p1)], inc=vp)
    if debug:
        P.op("vector", lambda v: v.tensor_copy(SF[:, 0:512], kvp[:]), inc=vp)
        P.op("gpsimd", lambda g: g.dma_start(out=dbg["d_kvraw"], in_=SF[:, 0:512]),
             waits=[(vp, vp.val)], inc=gld, inc_n=16)
    P.op("vector", lambda v: v.reciprocal(t8b[:], t8a[:]),
         waits=[(vp, vp.val)], inc=vp)
    P.op("vector", lambda v: v.memset(kvh[:], 0.0), waits=[(tp, t_p1)], inc=vp)
    for h in range(HEADS):
        q_ = h % 2
        j = h // 2
        P.op("vector", lambda v, q_=q_, j=j: v.tensor_single_scalar(
            kvh[q_ * 64:(q_ + 1) * 64, j * 128 + q_ * 64:j * 128 + (q_ + 1) * 64],
            kvp[q_ * 64:(q_ + 1) * 64, j * 64:(j + 1) * 64],
            t8b[q_ * 64:(q_ + 1) * 64, j:j + 1], ALU.mult),
            waits=[(vp, vp.val)] if h == 0 else (), inc=vp)
    v_kvh = vp.val
    if debug:
        P.op("gpsimd", lambda g: g.dma_start(out=dbg["d_kvh"], in_=kvh[:]),
             waits=[(vp, v_kvh)], inc=gld, inc_n=16)
        P.op("gpsimd", lambda g: g.dma_start(out=dbg["d_sumd"], in_=t8a[:]), inc=gld, inc_n=16)

    P.mark("p2")
    for c in range(8):
        P.op("sync", lambda s, c=c: s.dma_start(
            out=wqT[:, c, :], in_=wqkv_bf[0:1024, c * 128:(c + 1) * 128],
            transpose=True), waits=[(tp, t_p1), (gld, g_wqkv)], inc=tr, inc_n=16)
    for c in range(8):
        P.op("sync", lambda s, c=c: s.dma_start(
            out=woutT[:, c, :], in_=wout_bf[0:1024, c * 128:(c + 1) * 128],
            transpose=True), waits=[(gld, g_wout)], inc=tr, inc_n=16)
    y_wq = tr.val

    P.op("tensor", lambda tn: tn.matmul(BK[5][:], ones_bfr[0:1, 0:128],
                                        lng_row[0:1, 0:512], start=True, stop=True),
         waits=[(tp, t_p1), (vp, v_kvh), (sp, sp.val), (gld, g_lnrows), (tr, y_wq)], inc=tp)
    P.op("tensor", lambda tn: tn.matmul(BK[0][:], ones_bfr[0:1, 0:128],
                                        lng_row[0:1, 512:1024], start=True, stop=True), inc=tp)
    P.op("vector", lambda v: v.tensor_copy(lng_bc[0:128, 0:512], BK[5][:]),
         waits=[(tp, tp.val)], inc=vp)
    P.op("vector", lambda v: v.tensor_copy(lng_bc[0:128, 512:1024], BK[0][:]), inc=vp)
    P.op("tensor", lambda tn: tn.matmul(BK[5][:], ones_bfr[0:1, 0:128],
                                        lnb_row[0:1, 0:512], start=True, stop=True),
         waits=[(vp, vp.val)], inc=tp)
    P.op("tensor", lambda tn: tn.matmul(BK[0][:], ones_bfr[0:1, 0:128],
                                        lnb_row[0:1, 512:1024], start=True, stop=True), inc=tp)
    P.op("vector", lambda v: v.tensor_copy(lnb_bc[0:128, 0:512], BK[5][:]),
         waits=[(tp, tp.val)], inc=vp)
    P.op("vector", lambda v: v.tensor_copy(lnb_bc[0:128, 512:1024], BK[0][:]), inc=vp)
    v_lnbc = vp.val

    def ln_chain(out_ap, src_ap):
        P.op("vector", lambda v: v.bn_stats(bnbuf[:, 0:6], src_ap[:, 0:512]),
             waits=[(vp, vp.val)], inc=vp)
        P.op("vector", lambda v: v.bn_stats(bnbuf[:, 6:12], src_ap[:, 512:1024]), inc=vp)
        P.op("vector", lambda v: v.bn_aggr(mv2[:], bnbuf[:]),
             waits=[(vp, vp.val)], inc=vp)
        P.op("vector", lambda v: v.tensor_single_scalar(c1a[:], mv2[:, 1:2], EPS, ALU.add),
             waits=[(vp, vp.val)], inc=vp)
        P.op("scalar", lambda s: s.activation(c1b[:], c1a[:], AF.Sqrt),
             waits=[(vp, vp.val)], inc=sp)
        P.op("vector", lambda v: v.reciprocal(c1c[:], c1b[:]), waits=[(sp, sp.val)], inc=vp)
        P.op("vector", lambda v: v.tensor_scalar(c1d[:], mv2[:, 0:1], c1c[:], -1.0,
                                                 ALU.mult, ALU.mult),
             waits=[(vp, vp.val)], inc=vp)
        P.op("vector", lambda v: v.tensor_scalar(out_ap, src_ap, c1c[:], c1d[:],
                                                 ALU.mult, ALU.add),
             waits=[(vp, vp.val)], inc=vp)

    xt_free = {}
    y1st = {}
    gh_free = {}
    pab_free = {}
    sq_done = {}
    vgh_done = {}
    tattn = {}
    aev = {}
    y1v = {}
    yxt = {}

    def attn_stage(t):
        g_, tl = t // 4, t % 4
        pa, pb = (BK[2], BK[3]) if tl % 2 == 0 else (BK[4], BK[5])
        P.op("sync", lambda s, t=t: s.dma_start(
            out=x_t[t % 2], in_=x[t * 128:(t + 1) * 128, :]),
            waits=[(vp, xt_free.get(t - 2, 0))], inc=yld, inc_n=16)
        yxt[t] = yld.val
        def mk_attn(tensor, pa=pa, pb=pb, tl=tl):
            r = None
            for c in range(8):
                dst = pa if c < 4 else pb
                r = tensor.matmul(dst[:, (c % 4) * 128:(c % 4 + 1) * 128],
                                  q_cm[:, c, tl * 128:(tl + 1) * 128],
                                  kvh[:, c * 128:(c + 1) * 128],
                                  start=True, stop=True)
            return r
        P.op("tensor", mk_attn,
             waits=[(sp, sq_done[g_]), (vp, v_kvh), (vp, pab_free.get(t - 2, 0))], inc=tp)
        tattn[t] = tp.val
        buf = attn2[t % 2]
        P.op("scalar", lambda s, pa=pa, buf=buf: s.copy(buf[:, 0:512], pa[:]),
             waits=[(tp, tattn[t]), (vp, y1v.get(t - 2, 0))], inc=sp)
        P.op("scalar", lambda s, pb=pb, buf=buf: s.copy(buf[:, 512:1024], pb[:]), inc=sp)
        aev[t] = sp.val

    for g in range(NG):
        halfB = g >= 8
        if g == 8:
            for c in range(8):
                P.op("sync", lambda s, c=c: s.dma_start(
                    out=xcmA[:, c, :],
                    in_=x_bf[4096:8192, c * 128:(c + 1) * 128],
                    transpose=True), waits=[(tp, tp.val)], inc=tr, inc_n=16)
        y_xg = tr.val
        goff = (g % 8) * 512
        sq0 = sp.val
        for oc in range(8):
            def mk_q(tensor, oc=oc, goff=goff):
                for c in range(8):
                    tensor.matmul(BK[oc % 2][:], wqT[:, c, oc * 128:(oc + 1) * 128],
                                  xcmA[:, c, goff:goff + 512],
                                  start=(c == 0), stop=False)
                return tensor.matmul(BK[oc % 2][:], bqkv_bf[0:1, oc * 128:(oc + 1) * 128],
                                     ones_bfr[0:1, 0:512], start=False, stop=True)
            w = [(tr, y_xg if halfB else y_wq), (vp, v_lnbc)]
            w.append((sp, sq0 + oc - 1) if oc >= 2 else (sp, sq0))
            P.op("tensor", mk_q, waits=w, inc=tp)
            P.op("scalar", lambda s, oc=oc: s.activation(
                q_cm[:, oc, :], BK[oc % 2][:], AF.Silu),
                waits=[(tp, tp.val)], inc=sp)
        sq_done[g] = sp.val
        def mk_gh(tensor, goff=goff):
            for c in range(8):
                tensor.matmul(BK[6][0:64, :], wg1T[:, c * 64:(c + 1) * 64],
                              xcmA[:, c, goff:goff + 512],
                              start=(c == 0), stop=False)
            return tensor.matmul(BK[6][0:64, :], bg1_bf[0:1, :],
                                 ones_bfr[0:1, 0:512], start=False, stop=True)
        P.op("tensor", mk_gh, waits=[(vp, v_kvh), (vp, gh_free.get(g - 1, 0))], inc=tp)
        P.op("vector", lambda v: v.tensor_copy(gh_cm, BK[6][0:64, :]),
             waits=[(tp, tp.val)], inc=vp)
        v_gh = vp.val
        gh_free[g] = vp.val
        s_q = sp.val

        if debug and g == 0:
            P.op("gpsimd", lambda g: g.dma_start(
                out=dbg["d_q"], in_=q_cm[:]),
                waits=[(sp, s_q)], inc=gld, inc_n=16)

        attn_stage(4 * g)

        for tl in range(4):
            t = g * 4 + tl
            pa, pb = (BK[2], BK[3]) if tl % 2 == 0 else (BK[4], BK[5])
            buf = attn2[t % 2]
            P.op("vector", lambda v, t=t: v.engine_nop(), waits=[(sp, aev[t])], inc=vp)
            ln_chain(ln_t[:, :], buf[:, :])
            P.op("vector", lambda v: v.tensor_mul(ln_t2[:], ln_t[:], lng_bc[:]),
                 waits=[(vp, vp.val)], inc=vp)
            P.op("vector", lambda v: v.tensor_add(ln_t2[:], ln_t2[:], lnb_bc[:]),
                 waits=[(vp, vp.val)], inc=vp)
            v_ln = vp.val
            def mk_gate(tensor, pa=pa, pb=pb, tl=tl):
                r = None
                for hh, dst in ((0, pa), (1, pb)):
                    tensor.matmul(dst[:], gh_cm[:, tl * 128:(tl + 1) * 128],
                                  wg2T[0:64, hh * 512:(hh + 1) * 512],
                                  start=True, stop=False)
                    r = tensor.matmul(dst[:], ones_bfr[0:1, 0:128],
                                      bg2_bf[0:1, hh * 512:(hh + 1) * 512],
                                      start=False, stop=True)
                return r
            P.op("tensor", mk_gate, waits=[(sp, aev[t]), (vp, v_gh)], inc=tp)
            t_gate = tp.val
            if tl < 3:
                attn_stage(t + 1)
            for hh, src in ((0, pa), (1, pb)):
                P.op("scalar", lambda s, hh=hh, src=src: s.activation(
                    gate_t[:, hh * 512:(hh + 1) * 512], src[:], AF.Sigmoid),
                    waits=[(tp, t_gate)], inc=sp)
            s_gate = sp.val
            P.op("vector", lambda v: v.tensor_mul(g_t[:], ln_t2[:], gate_t[:]),
                 waits=[(sp, s_gate), (vp, vp.val)], inc=vp)
            v_gt = vp.val
            if debug and t == 0:
                P.op("gpsimd", lambda g, buf=buf: g.dma_start(out=dbg["d_attn"], in_=buf[:, :]),
                     waits=[(vp, v_ln)], inc=gld, inc_n=16)
                P.op("gpsimd", lambda g: g.dma_start(out=dbg["d_gate"], in_=gate_t[:]),
                     waits=[(vp, v_gt)], inc=gld, inc_n=16)
            def mk_tr(tensor):
                r = None
                for c in range(8):
                    r = tensor.transpose(PTR[:, c * 128:(c + 1) * 128],
                                         g_t[:, c * 128:(c + 1) * 128], ident_bf[:])
                return r
            P.op("tensor", mk_tr, waits=[(vp, v_gt)], inc=tp)
            P.op("scalar", lambda s: s.copy(
                g_cm[:].rearrange("p a b -> p (a b)"), PTR[:]),
                waits=[(tp, tp.val)], inc=sp)
            s_gcm = sp.val
            def mk_out(tensor, pa=pa, pb=pb):
                r = None
                for hh, dst in ((0, pa), (1, pb)):
                    for c in range(8):
                        tensor.matmul(dst[:], g_cm[:, c, :],
                                      woutT[:, c, hh * 512:(hh + 1) * 512],
                                      start=(c == 0), stop=False)
                    r = tensor.matmul(dst[:], ones_bfr[0:1, 0:128],
                                      bout_bf[0:1, hh * 512:(hh + 1) * 512],
                                      start=False, stop=True)
                return r
            P.op("tensor", mk_out, waits=[(sp, s_gcm)], inc=tp)
            t_out = tp.val
            P.op("vector", lambda v, pa=pa, t=t, buf=buf: v.tensor_add(
                buf[:, 0:512], x_t[t % 2][:, 0:512], pa[:]),
                waits=[(tp, t_out), (yld, yxt[t])], inc=vp)
            P.op("vector", lambda v, pb=pb, t=t, buf=buf: v.tensor_add(
                buf[:, 512:1024], x_t[t % 2][:, 512:1024], pb[:]), inc=vp)
            xt_free[t] = vp.val
            pab_free[t] = vp.val
            ln_chain(y1_t[:, :], buf[:, :])
            v_y1 = vp.val
            y1v[t] = vp.val
            P.op("scalar", lambda s, t=t: s.dma_start(
                out=y1_dram[t * 128:(t + 1) * 128, :], in_=y1_t[:]),
                waits=[(vp, v_y1), (scst, y1st.get(t - 1, 0))], inc=scst, inc_n=16)
            y1st[t] = scst.val
            if debug and t == 0:
                P.op("gpsimd", lambda g: g.dma_start(out=dbg["d_xt"], in_=x_t[0]),
                     waits=[(vp, v_y1)], inc=gld, inc_n=16)
                P.op("gpsimd", lambda g: g.dma_start(out=dbg["d_lnbc"], in_=lng_bc), inc=gld, inc_n=16)
                P.op("gpsimd", lambda g: g.dma_start(out=dbg["d_lnt"], in_=ln_t), inc=gld, inc_n=16)
                P.op("gpsimd", lambda g, buf=buf: g.dma_start(out=dbg["d_res"], in_=buf[:, :]), inc=gld, inc_n=16)
                P.op("gpsimd", lambda g: g.dma_start(out=dbg["d_lnt2"], in_=ln_t2[:]), inc=gld, inc_n=16)
                P.op("gpsimd", lambda g: g.dma_start(
                    out=dbg["d_gcm"], in_=g_cm[:].rearrange("p a b -> p (a b)")), inc=gld, inc_n=16)
                P.op("vector", lambda v: v.engine_nop(), waits=[(gld, gld.val)], inc=vp)
    t_p2 = tp.val
    v_p2 = vp.val
    s_p2 = sp.val
    sc_y1 = scst.val

    P.mark("p3")
    for half, dst in [(0, ycmA), (1, ycmB)]:
        for c in range(8):
            P.op("sync", lambda s, dst=dst, c=c, half=half: s.dma_start(
                out=dst[:, c, :],
                in_=y1_dram[half * 4096:(half + 1) * 4096, c * 128:(c + 1) * 128],
                transpose=True),
                waits=[(tp, t_p2), (scst, sc_y1), (vp, v_p2), (sp, s_p2)],
                inc=tr, inc_n=16)
    y_ycm = tr.val
    if debug:
        P.op("gpsimd", lambda g: g.dma_start(
            out=dbg["d_y1r"], in_=y1_dram[0:128, :]), waits=[(scst, sc_y1)], inc=gld, inc_n=16)
        P.op("gpsimd", lambda g: g.dma_start(
            out=dbg["d_ycm"], in_=ycmA[:, 0, 0:1024]), waits=[(tr, y_ycm)], inc=gld, inc_n=16)
        P.op("gpsimd", lambda g: g.dma_start(
            out=dbg["d_wgr"], in_=wgr[:]), inc=gld, inc_n=16)

    lg_free = {}
    for t in range(NT):
        ysrc = ycmA if t < 32 else ycmB
        toff = (t % 32) * 128
        pl = BK[2] if t % 2 == 0 else BK[3]
        def mk_lg(tensor, ysrc=ysrc, toff=toff, pl=pl):
            r = None
            for c in range(8):
                r = tensor.matmul(pl[:, 0:8], ysrc[:, c, toff:toff + 128],
                                  wgr[:, c * 8:(c + 1) * 8],
                                  start=(c == 0), stop=(c == 7))
            return r
        P.op("tensor", mk_lg,
             waits=[(tr, y_ycm), (vp, lg_free.get(t - 2, v_p2))], inc=tp)
        t_lg = tp.val
        if t == 31:
            t_lgA = tp.val
        P.op("vector", lambda v, pl=pl: v.tensor_copy(t8a[:], pl[:, 0:8]),
             waits=[(tp, t_lg)], inc=vp)
        lg_free[t] = vp.val
        if debug and t == 0:
            P.op("gpsimd", lambda g: g.dma_start(out=dbg["d_lg"], in_=t8a[:]),
                 waits=[(vp, vp.val)], inc=gld, inc_n=16)
        P.op("vector", lambda v: v.reduce_max(c1a[:], t8a[:], axis=AX.X),
             waits=[(vp, vp.val)], inc=vp)
        P.op("vector", lambda v: v.reduce_max(c1b[:], t8a[:], axis=AX.X, negate=True), inc=vp)
        P.op("scalar", lambda s: s.activation(t8b[:], t8a[:], AF.Exp, bias=c1b[:]),
             waits=[(vp, vp.val)], inc=sp)
        P.op("vector", lambda v: v.reduce_sum(c1c[:], t8b[:], axis=AX.X),
             waits=[(sp, sp.val)], inc=vp)
        P.op("vector", lambda v, t=t: v.reciprocal(gcol[:, t:t + 1], c1c[:]),
             waits=[(vp, vp.val)], inc=vp)
        P.op("vector", lambda v, t=t: v.tensor_scalar(
            mask_b[:, t * 8:(t + 1) * 8], t8a[:], c1a[:], None, ALU.is_equal),
            waits=[(vp, vp.val)], inc=vp)
        P.op("vector", lambda v, t=t: v.tensor_mul(
            t8c_b[:], mask_b[:, t * 8:(t + 1) * 8], iota8[:]),
            waits=[(vp, vp.val)], inc=vp)
        P.op("vector", lambda v, t=t: v.reduce_sum(ecol[:, t:t + 1], t8c_b[:], axis=AX.X),
             waits=[(vp, vp.val)], inc=vp)
        v_mk = vp.val
        P.op("tensor", lambda tensor, t=t: tensor.matmul(
            BK[6][0:8, t:t + 1], mask_b[:, t * 8:(t + 1) * 8], ones_colb[:],
            start=True, stop=True), waits=[(vp, v_mk)], inc=tp)
    t_tot = tp.val
    P.op("vector", lambda v: v.tensor_copy(tots[:, :], BK[6][0:8, 0:NT]),
         waits=[(tp, t_tot)], inc=vp)
    P.op("vector", lambda v: v.tensor_tensor_scan(
        offsx[:, :], tots[:, :], tots[:, :], 0.0, ALU.add, ALU.bypass),
        waits=[(vp, vp.val)], inc=vp)
    P.op("vector", lambda v: v.tensor_sub(offsx[:, :], offsx[:, :], tots[:, :]),
         waits=[(vp, vp.val)], inc=vp)
    v_offs = vp.val
    for e_ in range(8):
        P.op("gpsimd", lambda g, e_=e_: g.dma_start(
            out=offs_flat[0:1, :].rearrange("q (g r) -> q g r", r=8)[:, :, e_],
            in_=offsx[e_:e_ + 1, :], allow_slow_non_contiguous=True),
            waits=[(vp, v_offs), (gld, g_zero)], inc=gld, inc_n=16)
    g_offs = gld.val

    y_y1tm_half = {}
    for half, dst in [(0, y1tmA), (1, y1tmB)]:
        for q in range(4):
            P.op("sync", lambda s, dst=dst, half=half, q=q: s.dma_start(
                out=dst[:, q * 8:(q + 1) * 8, :],
                in_=y1_dram[half * 4096 + q * 1024: half * 4096 + (q + 1) * 1024, :]
                    .rearrange("(a p) d -> p a d", p=128)),
                waits=[(tp, t_lgA if half == 0 else t_tot)], inc=yld, inc_n=16)
        y_y1tm_half[half] = yld.val
    y_y1tm = yld.val

    incl_all = SF[:, 0:512]
    rkbuf = SF[:, 512:1024]
    r1_ = SF[:, 0:64]
    kept_ = SF[:, 64:128]
    s1_ = SF[:, 128:192]
    tr_ = SF[:, 192:256]

    def slot_batch(c0, c1):
        a, b = c0 * 8, c1 * 8
        P.op("vector", lambda v: v.tensor_copy(rkbuf[:, a:b], mask_b[:, a:b]),
             waits=[(vp, vp.val)], inc=vp)
        P.op("vector", lambda v: v.tensor_mul(rkbuf[:, a:b], rkbuf[:, a:b], incl_all[:, a:b]),
             waits=[(vp, vp.val)], inc=vp)
        P.op("vector", lambda v: v.reduce_sum(
            r1_[:, c0:c1].rearrange("p (t e) -> p t e", e=1),
            rkbuf[:, a:b].rearrange("p (t e) -> p t e", e=8), axis=AX.X),
            waits=[(vp, vp.val)], inc=vp)
        P.op("vector", lambda v: v.tensor_single_scalar(
            kept_[:, c0:c1], r1_[:, c0:c1], float(CAP), ALU.is_le),
            waits=[(vp, vp.val)], inc=vp)
        P.op("vector", lambda v: v.tensor_mul(kg128[:, c0:c1], gcol[:, c0:c1], kept_[:, c0:c1]),
             waits=[(vp, vp.val)], inc=vp)
        P.op("vector", lambda v: v.tensor_single_scalar(
            s1_[:, c0:c1], ecol[:, c0:c1], float(CAP), ALU.mult), inc=vp)
        P.op("vector", lambda v: v.tensor_add(s1_[:, c0:c1], s1_[:, c0:c1], r1_[:, c0:c1]),
             waits=[(vp, vp.val)], inc=vp)
        P.op("vector", lambda v: v.tensor_scalar(
            tr_[:, c0:c1], kept_[:, c0:c1], -float(TRASH + 1), float(TRASH),
            ALU.mult, ALU.add), inc=vp)
        P.op("vector", lambda v: v.tensor_mul(s1_[:, c0:c1], s1_[:, c0:c1], kept_[:, c0:c1]),
             waits=[(vp, vp.val)], inc=vp)
        P.op("vector", lambda v: v.tensor_add(s1_[:, c0:c1], s1_[:, c0:c1], tr_[:, c0:c1]),
             waits=[(vp, vp.val)], inc=vp)
        P.op("vector", lambda v: v.tensor_copy(slot128[:, c0:c1], s1_[:, c0:c1]),
             waits=[(vp, vp.val)], inc=vp)

    def idx_build(c0, c1):
        for r in range(8):
            P.op("gpsimd", lambda g, r=r, c0=c0, c1=c1: g.dma_start(
                out=idx16[0:16, :].rearrange("q (g r) -> q g r", r=8)[:, c0:c1, r],
                in_=slot128[r * 16:(r + 1) * 16, c0:c1],
                allow_slow_non_contiguous=True),
                waits=[(vp, vp.val)], inc=gld, inc_n=16)
        gA = gld.val
        for k in range(1, 8):
            P.op("gpsimd", lambda g, k=k, c0=c0, c1=c1: g.dma_start(
                out=idx16[k * 16:(k + 1) * 16, c0 * 8:c1 * 8],
                in_=idx16[0:16, c0 * 8:c1 * 8]),
                waits=[(gld, gA)], inc=gld, inc_n=16)

    cum_free = {}
    g_scat_half = {}
    for t in range(NT):
        pc = BK[2] if t % 2 == 0 else BK[4]
        po = BK[3] if t % 2 == 0 else BK[5]
        def mk_cum(tensor, t=t, pc=pc, po=po):
            tensor.matmul(pc[:, 0:8], ut_bf[:], mask_b[:, t * 8:(t + 1) * 8],
                          start=True, stop=True)
            return tensor.matmul(po[:, 0:8], ones_f32r[0:1, 0:128],
                                 offs_flat[0:1, t * 8:(t + 1) * 8],
                                 start=True, stop=True)
        P.op("tensor", mk_cum,
             waits=[(gld, g_offs), (vp, cum_free.get(t - 2, v_offs))], inc=tp)
        t_cum = tp.val
        P.op("vector", lambda v, pc=pc: v.tensor_copy(t8a[:], pc[:, 0:8]),
             waits=[(tp, t_cum), (vp, vp.val)], inc=vp)
        P.op("vector", lambda v, po=po, t=t: v.tensor_add(
            incl_all[:, t * 8:(t + 1) * 8], t8a[:], po[0:128, 0:8]),
            waits=[(vp, vp.val)], inc=vp)
        cum_free[t] = vp.val
        if t == 31:
            slot_batch(0, 32)
            idx_build(0, 32)
            g_idxA = gld.val
            P.op("gpsimd", lambda g: g.dma_scatter_add(
                ei_dram, y1tmA[:], idx16[:, 0:256], 4096, 4096, D),
                waits=[(gld, g_idxA), (yld, y_y1tm_half[0])], inc=gsc, inc_n=16)
            g_scat_half[0] = gsc.val
    slot_batch(32, 64)
    idx_build(32, 64)
    g_idx = gld.val
    v_slot = vp.val
    if debug:
        P.op("scalar", lambda s: s.dma_start(out=dbg["d_idx"], in_=idx16[:]),
             waits=[(gld, g_idx)], inc=scst, inc_n=16)
        P.op("vector", lambda v: v.tensor_copy(SF[:, 1024:1088], slot128[:, :]),
             waits=[(vp, v_slot)], inc=vp)
        P.op("scalar", lambda s: s.dma_start(out=dbg["d_slot"], in_=SF[:, 1024:1088]),
             waits=[(vp, vp.val)], inc=scst, inc_n=16)
        P.op("scalar", lambda s: s.dma_start(out=dbg["d_kg"], in_=kg128[:]), inc=scst, inc_n=16)

    P.op("gpsimd", lambda g: g.dma_scatter_add(
        ei_dram, y1tmB[:], idx16[:, 256:512], 4096, 4096, D),
        waits=[(gld, g_idx), (yld, y_y1tm)], inc=gsc, inc_n=16)
    g_scatA = g_scat_half[0]
    g_scat = gsc.val

    t_w1_done = {}
    t_w2_done = {}
    w1ck = {}
    w2ck = {}
    for ex in range(NE):
        if ex == 0:
            P.op("gpsimd", lambda g: g.dma_start(
                out=w1sb[:], in_=w1[0].rearrange("(c p) h -> p c h", p=128)),
                waits=[(gsc, g_scatA)], inc=gld, inc_n=16)
            g_w1 = gld.val
            P.op("gpsimd", lambda g: g.dma_start(
                out=w2sb[:], in_=w2[0].rearrange("(c p) d -> p c d", p=128)),
                waits=[(gsc, g_scat)], inc=gld, inc_n=16)
            g_w2 = gld.val
        eiw = [(gsc, g_scat)]
        if ex > 0:
            eiw.append((tp, t_w1_done[ex - 1]))
        for c in range(8):
            P.op("sync", lambda s, ex=ex, c=c: s.dma_start(
                out=ei_cm[:, c, :],
                in_=ei_dram[ex * 1024:(ex + 1) * 1024, c * 128:(c + 1) * 128],
                transpose=True), waits=eiw if c == 0 else (), inc=tr, inc_n=16)
        y_ei = tr.val

        for sc in range(4):
            eob = eo_sb[:, (sc % 2) * 2:(sc % 2) * 2 + 2, :]
            s40 = sp.val
            gelu_done = {}
            def mk_w2(tensor, hc, sc=sc):
                r = None
                for sub in range(2):
                    for dh in range(2):
                        r = tensor.matmul(BK[3 + sub * 2 + dh][:],
                                          h_ring[:, hc % 4, sub * 128:(sub + 1) * 128],
                                          w2sb[:, hc, dh * 512:(dh + 1) * 512],
                                          start=(hc == 0), stop=(hc == 31))
                return r
            for hc in range(32):
                bw = BK[hc % 3]
                def mk_w1(tensor, hc=hc, sc=sc, bw=bw):
                    r = None
                    for c in range(8):
                        r = tensor.matmul(bw[:, 0:256], w1sb[:, c, hc * 128:(hc + 1) * 128],
                                          ei_cm[:, c, sc * 256:(sc + 1) * 256],
                                          start=(c == 0), stop=(c == 7))
                    return r
                if ex == 0:
                    w = [(gld, g_w1), (tr, y_ei)]
                else:
                    w = [(gld, w1ck[(ex, hc)]), (tr, y_ei)]
                if hc >= 3:
                    w.append((sp, s40 + hc - 2))
                if hc == 0 and sc > 0:
                    w.append((vp, vp.val))
                P.op("tensor", mk_w1, waits=w, inc=tp)
                t_w1hc = tp.val
                if sc == 3 and ex < NE - 1:
                    P.op("gpsimd", lambda g, ex=ex, hc=hc: g.dma_start(
                        out=w1sb[:, :, hc * 128:(hc + 1) * 128],
                        in_=w1[ex + 1].rearrange("(c p) h -> p c h", p=128)[:, :, hc * 128:(hc + 1) * 128]),
                        waits=[(tp, t_w1hc)], inc=gld, inc_n=16)
                    w1ck[(ex + 1, hc)] = gld.val
                P.op("scalar", lambda s, hc=hc, bw=bw: s.activation(
                    h_ring[:, hc % 4, :], bw[:, 0:256], AF.Gelu_apprx_tanh),
                    waits=[(tp, t_w1hc)], inc=sp)
                gelu_done[hc] = sp.val
                if hc >= 1:
                    w2w = [(sp, gelu_done[hc - 1])]
                    w2w.append((gld, g_w2) if ex == 0 else (gld, w2ck[(ex, hc - 1)]))
                    P.op("tensor", lambda tensor, hc=hc: mk_w2(tensor, hc - 1),
                         waits=w2w, inc=tp)
                    if sc == 3 and ex < NE - 1:
                        P.op("gpsimd", lambda g, ex=ex, hc=hc: g.dma_start(
                            out=w2sb[:, hc - 1, :],
                            in_=w2[ex + 1].rearrange("(c p) d -> p c d", p=128)[:, hc - 1, :]),
                            waits=[(tp, tp.val)], inc=gld, inc_n=16)
                        w2ck[(ex + 1, hc - 1)] = gld.val
            P.op("tensor", lambda tensor: mk_w2(tensor, 31),
                 waits=[(sp, gelu_done[31])] +
                       ([(gld, g_w2)] if ex == 0 else [(gld, w2ck[(ex, 31)])]), inc=tp)
            if sc == 3 and ex < NE - 1:
                P.op("gpsimd", lambda g, ex=ex: g.dma_start(
                    out=w2sb[:, 31, :],
                    in_=w2[ex + 1].rearrange("(c p) d -> p c d", p=128)[:, 31, :]),
                    waits=[(tp, tp.val)], inc=gld, inc_n=16)
                w2ck[(ex + 1, 31)] = gld.val
            t_w2sc = tp.val
            if sc == 3:
                t_w1_done[ex] = tp.val
                t_w2_done[ex] = tp.val
            for sub in range(2):
                for dh in range(2):
                    P.op("vector", lambda v, sub=sub, dh=dh, eob=eob: v.tensor_copy(
                        eob[:, sub, dh * 512:(dh + 1) * 512], BK[3 + sub * 2 + dh][:]),
                        waits=[(tp, t_w2sc)], inc=vp)
            v_eo = vp.val
            P.op("scalar", lambda s, ex=ex, sc=sc, eob=eob: s.dma_start(
                out=eo_dram[ex * 1024 + sc * 256: ex * 1024 + (sc + 1) * 256, :]
                    .rearrange("(a p) d -> p a d", p=128),
                in_=eob), waits=[(vp, v_eo)], inc=scst, inc_n=16)
    sc_eo = scst.val

    P.mark("p5")
    ost = {}
    for ch in range(8):
        gw = [(scst, sc_eo), (gld, g_idx)]
        yw = [(scst, sc_eo)]
        if ch > 0:
            gw.append((vp, vp.val))
            yw.append((vp, vp.val))
        P.op("gpsimd", lambda g, ch=ch: g.dma_gather(
            of_sb[:], eo_dram, idx16[:, ch * 64:(ch + 1) * 64], 1024, 1024, D),
            waits=gw, inc=gld, inc_n=16)
        g_of = gld.val
        P.op("sync", lambda s, ch=ch: s.dma_start(
            out=y1c[:], in_=y1_dram[ch * 1024:(ch + 1) * 1024, :]
                .rearrange("(a p) d -> p a d", p=128)),
            waits=yw, inc=yld, inc_n=16)
        y_y1c = yld.val
        for tl in range(8):
            t = ch * 8 + tl
            ob = outstage[tl % 2]
            P.op("vector", lambda v, tl=tl, t=t: v.scalar_tensor_tensor(
                res_f[:, :], of_sb[:, tl, :], kg128[:, t:t + 1], y1c[:, tl, :],
                ALU.mult, ALU.add),
                waits=[(gld, g_of), (yld, y_y1c), (scst, ost.get(t - 2, 0))], inc=vp)
            ln_chain(ob[:, :], res_f[:, :])
            v_out = vp.val
            P.op("scalar", lambda s, t=t, ob=ob: s.dma_start(
                out=y[t * 128:(t + 1) * 128, :], in_=ob[:, :]),
                waits=[(vp, v_out)], inc=scst, inc_n=16)
            ost[t] = scst.val
    sc_out = scst.val

    P.op("sync", lambda s: s.wait_ge(scst.h, sc_out))
    P.op("gpsimd", lambda g: g.wait_ge(scst.h, sc_out))

    if isinstance(max_items, str):
        max_items = P.marks[max_items]
    import os
    if max_items is None and os.environ.get("KMAXI"):
        mi = os.environ["KMAXI"]
        max_items = P.marks[mi] if mi in P.marks else int(mi)
    with nc.Block() as block:
        @block.sync
        def _(eng):
            P.emit("sync", eng, max_items)

        @block.gpsimd
        def _(eng):
            P.emit("gpsimd", eng, max_items)

        @block.tensor
        def _(eng):
            P.emit("tensor", eng, max_items)

        @block.vector
        def _(eng):
            P.emit("vector", eng, max_items)

        @block.scalar
        def _(eng):
            P.emit("scalar", eng, max_items)
    import sys
    print(f"[kernel] program items={len(P.items)} marks={P.marks} max_items={max_items}",
          file=sys.stderr)

    st.close()
    library_overlay.lower_extended_insts(nc)
    return nc


_NC_CACHE = {}


def _get_nc(debug=False):
    key = bool(debug)
    if key not in _NC_CACHE:
        _NC_CACHE[key] = build_nc(debug=debug)
    return _NC_CACHE[key]


def kernel(L0_attn_x, L0_attn_w_qkv, L0_attn_b_qkv, L0_attn_w_g1, L0_attn_b_g1,
           L0_attn_w_g2, L0_attn_b_g2, L0_attn_w_out, L0_attn_b_out,
           L0_attn_ln_g, L0_attn_ln_b, L0_ffn_Wg, L0_ffn_W1, L0_ffn_W2):
    nc = _get_nc(debug=False)
    common = {
        "w_qkv": np.ascontiguousarray(L0_attn_w_qkv, dtype=np.float32),
        "b_qkv": np.ascontiguousarray(L0_attn_b_qkv, dtype=np.float32).reshape(1, -1),
        "w_g1": np.ascontiguousarray(L0_attn_w_g1, dtype=np.float32),
        "b_g1": np.ascontiguousarray(L0_attn_b_g1, dtype=np.float32).reshape(1, -1),
        "w_g2": np.ascontiguousarray(L0_attn_w_g2, dtype=np.float32),
        "b_g2": np.ascontiguousarray(L0_attn_b_g2, dtype=np.float32).reshape(1, -1),
        "w_out": np.ascontiguousarray(L0_attn_w_out, dtype=np.float32),
        "b_out": np.ascontiguousarray(L0_attn_b_out, dtype=np.float32).reshape(1, -1),
        "ln_g": np.ascontiguousarray(L0_attn_ln_g, dtype=np.float32).reshape(1, -1),
        "ln_b": np.ascontiguousarray(L0_attn_ln_b, dtype=np.float32).reshape(1, -1),
        "wg": np.ascontiguousarray(L0_ffn_Wg, dtype=np.float32),
        "w1": np.ascontiguousarray(L0_ffn_W1, dtype=np.float32),
        "w2": np.ascontiguousarray(L0_ffn_W2, dtype=np.float32),
    }
    xs = np.ascontiguousarray(L0_attn_x, dtype=np.float32)
    in_maps = [dict(common, x=np.ascontiguousarray(xs[b])) for b in range(8)]
    res = run_bass_kernel_spmd(nc, in_maps, core_ids=list(range(8)))
    out = np.stack([np.asarray(res.results[b]["y"]) for b in range(8)], axis=0)
    return out.astype(np.float32)


# version 05 (final)
# speedup vs baseline: 1.0092x; 1.0092x over previous
import numpy as np
import concourse.bass as bass
import concourse.mybir as mybir
import concourse.library_overlay as library_overlay
from concourse.bass_utils import run_bass_kernel_spmd
from concourse.library_config import mlp as mlp_lib
from contextlib import ExitStack

F32 = mybir.dt.float32
BF16 = mybir.dt.bfloat16
I16 = mybir.dt.int16
AF = mybir.ActivationFunctionType
ALU = mybir.AluOpType
AX = mybir.AxisListType

T = 8192
D = 1024
NT = T // 128
NG = T // 512
NE = 8
CAP = 1024
HEADS = 16
EPS = 1e-5
TRASH = 8192


class Sem:
    def __init__(self, h):
        self.h = h
        self.val = 0

    def bump(self, n):
        self.val += n
        return self.val


class Prog:
    def __init__(self):
        self.items = []
        self._last_wait = {}
        self.marks = {}

    def mark(self, name):
        self.marks[name] = len(self.items)

    def op(self, eng, fn, waits=(), inc=None, inc_n=1):
        ws = []
        for (s, v) in waits:
            if v <= 0:
                continue
            k = (eng, id(s))
            if self._last_wait.get(k, -1) >= v:
                continue
            self._last_wait[k] = v
            ws.append((s, v))
        if inc is not None:
            inc.bump(inc_n)
        self.items.append((eng, fn, ws, inc, inc_n))

    def emit(self, eng_name, eng, max_items=None):
        items = self.items if max_items is None else self.items[:max_items]
        for (e, fn, ws, inc, inc_n) in items:
            if e != eng_name:
                continue
            for (s, v) in ws:
                eng.wait_ge(s.h, v)
            r = fn(eng)
            if inc is not None:
                if isinstance(r, list):
                    r = r[-1]
                assert r is not None, "op with inc returned no instruction"
                r.then_inc(inc.h, inc_n)


def build_nc(debug=False, max_items=None):
    nc = bass.Bass("TRN2", target_bir_lowering=False)

    x = nc.dram_tensor("x", [T, D], F32, kind="ExternalInput").ap()
    w_qkv = nc.dram_tensor("w_qkv", [3 * D, D], F32, kind="ExternalInput").ap()
    b_qkv = nc.dram_tensor("b_qkv", [1, 3 * D], F32, kind="ExternalInput").ap()
    w_g1 = nc.dram_tensor("w_g1", [64, D], F32, kind="ExternalInput").ap()
    b_g1 = nc.dram_tensor("b_g1", [1, 64], F32, kind="ExternalInput").ap()
    w_g2 = nc.dram_tensor("w_g2", [D, 64], F32, kind="ExternalInput").ap()
    b_g2 = nc.dram_tensor("b_g2", [1, D], F32, kind="ExternalInput").ap()
    w_out = nc.dram_tensor("w_out", [D, D], F32, kind="ExternalInput").ap()
    b_out = nc.dram_tensor("b_out", [1, D], F32, kind="ExternalInput").ap()
    ln_g = nc.dram_tensor("ln_g", [1, D], F32, kind="ExternalInput").ap()
    ln_b = nc.dram_tensor("ln_b", [1, D], F32, kind="ExternalInput").ap()
    wg = nc.dram_tensor("wg", [D, NE], F32, kind="ExternalInput").ap()
    w1 = nc.dram_tensor("w1", [NE, D, 4 * D], F32, kind="ExternalInput").ap()
    w2 = nc.dram_tensor("w2", [NE, 4 * D, D], F32, kind="ExternalInput").ap()

    y = nc.dram_tensor("y", [T, D], F32, kind="ExternalOutput").ap()

    x_bf = nc.dram_tensor("x_bf", [T, D], BF16, kind="Internal").ap()
    wqkv_bf = nc.dram_tensor("wqkv_bf", [3 * D, D], BF16, kind="Internal").ap()
    wout_bf = nc.dram_tensor("wout_bf", [D, D], BF16, kind="Internal").ap()
    y1_dram = nc.dram_tensor("y1_dram", [T, D], BF16, kind="Internal").ap()
    ei_dram = nc.dram_tensor("ei_dram", [T + 128, D], BF16, kind="Internal").ap()
    eo_dram = nc.dram_tensor("eo_dram", [T + 128, D], BF16, kind="Internal").ap()

    dbg = {}
    if debug:
        for name, shape, dt in [
            ("d_kvh", [128, 1024], F32), ("d_sumd", [128, 8], F32),
            ("d_lg", [128, 8], F32), ("d_slot", [128, NT], F32),
            ("d_kg", [128, NT], F32), ("d_attn", [128, D], F32),
            ("d_gate", [128, D], F32), ("d_idx", [128, 512], I16),
            ("d_q", [128, 8, 512], F32),
            ("d_expk0", [128, D], F32), ("d_expk63", [128, D], F32),
            ("d_v0", [128, D], F32), ("d_kvraw", [128, 512], F32),
            ("d_expk8", [128, D], F32), ("d_y1r", [128, D], F32),
            ("d_ycm", [128, 1024], F32), ("d_wgr", [128, 64], F32),
            ("d_xt", [128, D], F32), ("d_res", [128, D], F32),
            ("d_lnt2", [128, D], F32), ("d_gcm", [128, 1024], F32),
            ("d_lnbc", [128, D], F32), ("d_lnt", [128, D], F32),
        ]:
            dbg[name] = nc.dram_tensor(name, shape, dt, kind="ExternalOutput").ap()

    st = ExitStack()
    e = st.enter_context

    R1 = e(nc.sbuf_tensor("R1", [128, 32768], BF16))
    R2 = e(nc.sbuf_tensor("R2", [128, 32768], BF16))
    SB = e(nc.sbuf_tensor("SB", [128, 20480], BF16))
    SF = e(nc.sbuf_tensor("SF", [128, 4096], F32))

    bqkv_bf = e(nc.sbuf_tensor("bqkv_bf", [1, 3 * D], BF16))
    bg1_bf = e(nc.sbuf_tensor("bg1_bf", [1, 64], BF16))
    bg2_bf = e(nc.sbuf_tensor("bg2_bf", [1, D], BF16))
    bout_bf = e(nc.sbuf_tensor("bout_bf", [1, D], BF16))
    ones_bfr = e(nc.sbuf_tensor("ones_bfr", [1, 512], BF16))
    ones_f32r = e(nc.sbuf_tensor("ones_f32r", [1, 128], F32))
    ones_colb = e(nc.sbuf_tensor("ones_colb", [128, 1], BF16))
    ident_bf = e(nc.sbuf_tensor("ident_bf", [128, 128], BF16))
    ut_bf = e(nc.sbuf_tensor("ut_bf", [128, 128], BF16))
    iota8 = e(nc.sbuf_tensor("iota8", [128, 8], BF16))
    wgr = e(nc.sbuf_tensor("wgr", [128, 8 * NE], BF16))
    wg1T = e(nc.sbuf_tensor("wg1T", [128, 8 * 64], BF16))
    wg2T = e(nc.sbuf_tensor("wg2T", [64, D], BF16))

    mask_b = e(nc.sbuf_tensor("mask_b", [128, NT * 8], BF16))
    gcol = e(nc.sbuf_tensor("gcol", [128, NT], F32))
    ecol = e(nc.sbuf_tensor("ecol", [128, NT], F32))
    kg128 = e(nc.sbuf_tensor("kg128", [128, NT], F32))
    slot128 = e(nc.sbuf_tensor("slot128", [128, NT], I16))
    idx16 = e(nc.sbuf_tensor("idx16", [128, 512], I16))
    tots = e(nc.sbuf_tensor("tots", [8, NT], F32))
    offsx = e(nc.sbuf_tensor("offsx", [8, NT], F32))
    offs_flat = e(nc.sbuf_tensor("offs_flat", [1, 512], F32))
    t8a = e(nc.sbuf_tensor("t8a", [128, 8], F32))
    t8b = e(nc.sbuf_tensor("t8b", [128, 8], F32))
    t8c = e(nc.sbuf_tensor("t8c", [128, 8], F32))
    t8c_b = e(nc.sbuf_tensor("t8c_b", [128, 8], BF16))
    c1a = e(nc.sbuf_tensor("c1a", [128, 1], F32))
    c1b = e(nc.sbuf_tensor("c1b", [128, 1], F32))
    c1c = e(nc.sbuf_tensor("c1c", [128, 1], F32))
    c1d = e(nc.sbuf_tensor("c1d", [128, 1], F32))
    bnbuf = e(nc.sbuf_tensor("bnbuf", [128, 12], F32))
    mv2 = e(nc.sbuf_tensor("mv2", [128, 2], F32))

    BK = [e(nc.psum_tensor(f"bk{i}", [128, 512], F32)) for i in range(7)]
    PTR = e(nc.psum_tensor("ptr", [128, 1024], BF16))

    def bfview(region, off, shape):
        n = 1
        for s_ in shape[1:]:
            n *= s_
        apv = region[:, off:off + n]
        if len(shape) == 3:
            return apv.rearrange("p (a b) -> p a b", a=shape[1])
        return apv

    xcmA = bfview(R1, 0, [128, 8, 4096])
    xcmB = bfview(R2, 0, [128, 8, 4096])
    wkvT = bfview(SB, 0, [128, 8, 2048])
    expk = SB[:, 16384:17408]
    v_t = SB[:, 17408:18432]
    g2nat = bfview(SB, 18432, [128, 8, 64])
    g1nat = SB[0:64, 18944:19968]
    lng_row = SB[0:1, 18432:19456]
    lnb_row = SB[0:1, 19456:20480]

    wqT = bfview(R2, 0, [128, 8, 1024])
    woutT = bfview(R2, 8192, [128, 8, 1024])
    q_cm = bfview(R2, 16384, [128, 8, 512])
    gh_cm = R2[0:64, 20480:20992]
    ln_t = R2[:, 20992:22016]
    ln_t2 = R2[:, 22016:23040]
    gate_t = R2[:, 23040:24064]
    g_t = R2[:, 24064:25088]
    g_cm = bfview(R2, 25088, [128, 8, 128])
    y1_t = R2[:, 26112:27136]
    lng_bc = R2[:, 27136:28160]
    lnb_bc = R2[:, 28160:29184]
    kvh = R2[:, 29184:30208]

    ycmA = bfview(R1, 0, [128, 8, 4096])
    ycmB = bfview(R2, 0, [128, 8, 4096])
    y1tmA = bfview(R1, 0, [128, 32, 1024])
    y1tmB = bfview(R2, 0, [128, 32, 1024])
    w1sb = bfview(R1, 0, [128, 8, 4096])
    w2sb = bfview(R2, 0, [128, 32, 1024])
    ei_cm = bfview(SB, 0, [128, 8, 1024])
    h_ring = bfview(SB, 8192, [128, 4, 256])
    eo_sb = bfview(SB, 9216, [128, 4, 1024])
    of_sb = bfview(SB, 0, [128, 8, 1024])
    y1c = bfview(SB, 8192, [128, 8, 1024])
    zsrc = SB[:, 16384:18432]

    x_t = [SF[:, 0:1024], SF[:, 1024:2048]]
    iotajp = SF[:, 0:128]
    attn2 = [SF[:, 2048:3072], SF[:, 3072:4096]]
    res_f = attn2[0]
    outstage = [SF[:, 0:1024], SF[:, 1024:2048]]

    sems = {}
    for name in ["gld", "yld", "tr", "scst", "tp", "vp", "sp", "gsc"]:
        sems[name] = Sem(e(nc.semaphore(name)))
    gld, yld, tr, scst, tp, vp, sp, gsc = (sems[k] for k in
        ["gld", "yld", "tr", "scst", "tp", "vp", "sp", "gsc"])

    P = Prog()

    P.op("gpsimd", lambda g: g.iota(iota8[:], [[1, 8]], base=0, channel_multiplier=0,
                                    allow_small_or_imprecise_dtypes=True), inc=gld)
    P.op("gpsimd", lambda g: g.iota(iotajp[:], [[1, 128]], base=0, channel_multiplier=-1,
                                    allow_small_or_imprecise_dtypes=True), inc=gld)
    g_iota = gld.val
    P.op("gpsimd", lambda g: g.load_library(mlp_lib))
    P.op("vector", lambda v: v.tensor_single_scalar(ident_bf[:], iotajp[:], 0.0, ALU.is_equal),
         waits=[(gld, g_iota)], inc=vp)
    P.op("vector", lambda v: v.tensor_single_scalar(ut_bf[:], iotajp[:], 0.0, ALU.is_ge), inc=vp)
    P.op("vector", lambda v: v.memset(ones_bfr[:], 1.0), inc=vp)
    P.op("vector", lambda v: v.memset(ones_f32r[:], 1.0), inc=vp)
    P.op("vector", lambda v: v.memset(ones_colb[:], 1.0), inc=vp)
    P.op("vector", lambda v: v.memset(zsrc[:], 0.0), inc=vp)
    P.op("vector", lambda v: v.memset(idx16[:], 0), inc=vp)
    v_zero = vp.val

    P.op("gpsimd", lambda g: g.dma_start(out=x_bf[0:4096, :], in_=x[0:4096, :]), inc=gld, inc_n=16)
    g_xa = gld.val
    P.op("gpsimd", lambda g: g.dma_start(out=x_bf[4096:8192, :], in_=x[4096:8192, :]), inc=gld, inc_n=16)
    g_xb = gld.val
    P.op("gpsimd", lambda g: g.dma_start(out=wqkv_bf, in_=w_qkv), inc=gld, inc_n=16)
    g_wqkv = gld.val
    P.op("gpsimd", lambda g: g.dma_start(out=wout_bf, in_=w_out), inc=gld, inc_n=16)
    g_wout = gld.val
    P.op("gpsimd", lambda g: g.dma_start(out=bqkv_bf[:], in_=b_qkv), inc=gld, inc_n=16)
    P.op("gpsimd", lambda g: g.dma_start(out=bg1_bf[:], in_=b_g1), inc=gld, inc_n=16)
    P.op("gpsimd", lambda g: g.dma_start(out=bg2_bf[:], in_=b_g2), inc=gld, inc_n=16)
    P.op("gpsimd", lambda g: g.dma_start(out=bout_bf[:], in_=b_out), inc=gld, inc_n=16)
    P.op("gpsimd", lambda g: g.dma_start(
        out=wgr[:].rearrange("p (c e) -> p c e", e=NE),
        in_=wg.rearrange("(c p) e -> p c e", p=128),
        allow_slow_non_contiguous=True), inc=gld, inc_n=16)
    P.op("gpsimd", lambda g: g.dma_start(out=g1nat, in_=w_g1), inc=gld, inc_n=16)
    P.op("gpsimd", lambda g: g.dma_start(
        out=g2nat[:], in_=w_g2.rearrange("(c p) d -> p c d", p=128)), inc=gld, inc_n=16)
    g_smallw = gld.val

    def mk_g1tr(tensor):
        r = None
        for c in range(8):
            r = tensor.transpose(PTR[:, c * 64:(c + 1) * 64],
                                 g1nat[:, c * 128:(c + 1) * 128], ident_bf[0:64, 0:64])
        return r
    P.op("tensor", mk_g1tr, waits=[(gld, g_smallw), (vp, v_zero)], inc=tp)
    P.op("vector", lambda v: v.tensor_copy(wg1T[:], PTR[:, 0:512]),
         waits=[(tp, tp.val)], inc=vp)
    v_g1 = vp.val
    def mk_g2tr(tensor):
        r = None
        for c in range(8):
            r = tensor.transpose(PTR[0:64, c * 128:(c + 1) * 128],
                                 g2nat[:, c, :], ident_bf[:])
        return r
    P.op("tensor", mk_g2tr, waits=[(vp, v_g1)], inc=tp)
    P.op("vector", lambda v: v.tensor_copy(wg2T[:], PTR[0:64, :]),
         waits=[(tp, tp.val)], inc=vp)
    v_pro0 = vp.val
    t_protr = tp.val
    P.op("gpsimd", lambda g: g.dma_start(out=lng_row, in_=ln_g),
         waits=[(tp, t_protr)], inc=gld, inc_n=16)
    P.op("gpsimd", lambda g: g.dma_start(out=lnb_row, in_=ln_b), inc=gld, inc_n=16)
    g_lnrows = gld.val

    for ch in range(32):
        P.op("gpsimd", lambda g, ch=ch: g.dma_start(
            out=ei_dram[ch * 256:(ch + 1) * 256, :].rearrange("(a p) d -> p a d", p=128),
            in_=zsrc[:].rearrange("p (a d) -> p a d", d=1024)),
            waits=[(vp, v_zero)], inc=gld, inc_n=16)
    P.op("gpsimd", lambda g: g.dma_start(
        out=ei_dram[8192:8320, :], in_=zsrc[:, 0:1024]), inc=gld, inc_n=16)
    P.op("gpsimd", lambda g: g.dma_start(
        out=eo_dram[8192:8320, :], in_=zsrc[:, 0:1024]), inc=gld, inc_n=16)
    g_zero = gld.val

    for half, (dst, gw) in enumerate([(xcmA, g_xa), (xcmB, g_xb)]):
        for c in range(8):
            P.op("sync", lambda s, dst=dst, c=c, half=half: s.dma_start(
                out=dst[:, c, :],
                in_=x_bf[half * 4096:(half + 1) * 4096, c * 128:(c + 1) * 128],
                transpose=True), waits=[(gld, gw)], inc=tr, inc_n=16)
    y_xcm = tr.val
    for kv_i in range(2):
        for c in range(8):
            P.op("sync", lambda s, kv_i=kv_i, c=c: s.dma_start(
                out=wkvT[:, c, kv_i * 1024:(kv_i + 1) * 1024],
                in_=wqkv_bf[(1 + kv_i) * 1024:(2 + kv_i) * 1024, c * 128:(c + 1) * 128],
                transpose=True), waits=[(gld, g_wqkv)], inc=tr, inc_n=16)
    y_wkv = tr.val

    P.mark("p1")
    kvp = BK[6]
    sumdp = BK[4]
    v_p1base = vp.val
    s_p1base = sp.val
    for t in range(NT):
        half = xcmA if t < 32 else xcmB
        toff = (t % 32) * 128
        tw = [(tr, y_xcm), (tr, y_wkv), (gld, g_smallw), (vp, v_p1base)]
        if t > 0:
            tw += [(sp, s_p1base + 2 * t), (vp, v_p1base + 2 * t)]
        for hh in range(2):
            def mk_k(tensor, half=half, toff=toff, hh=hh):
                for c in range(8):
                    tensor.matmul(BK[hh][:], half[:, c, toff:toff + 128],
                                  wkvT[:, c, hh * 512:(hh + 1) * 512],
                                  start=(c == 0), stop=False)
                return tensor.matmul(BK[hh][:], ones_bfr[0:1, 0:128],
                                     bqkv_bf[0:1, 1024 + hh * 512:1024 + (hh + 1) * 512],
                                     start=False, stop=True)
            P.op("tensor", mk_k, waits=tw if hh == 0 else (), inc=tp)
        for hh in range(2):
            def mk_v(tensor, half=half, toff=toff, hh=hh):
                for c in range(8):
                    tensor.matmul(BK[2 + hh][:], half[:, c, toff:toff + 128],
                                  wkvT[:, c, 1024 + hh * 512:1024 + (hh + 1) * 512],
                                  start=(c == 0), stop=False)
                return tensor.matmul(BK[2 + hh][:], ones_bfr[0:1, 0:128],
                                     bqkv_bf[0:1, 2048 + hh * 512:2048 + (hh + 1) * 512],
                                     start=False, stop=True)
            P.op("tensor", mk_v, inc=tp)
        base = tp.val - 4
        for hh in range(2):
            P.op("scalar", lambda s, hh=hh: s.activation(
                expk[:, hh * 512:(hh + 1) * 512], BK[hh][:], AF.Exp),
                waits=[(tp, base + 1 + hh)], inc=sp)
        for hh in range(2):
            P.op("vector", lambda v, hh=hh: v.tensor_copy(
                v_t[:, hh * 512:(hh + 1) * 512], BK[2 + hh][:]),
                waits=[(tp, base + 3 + hh)], inc=vp)
        def mk_kv(tensor, t=t):
            r = None
            for h in range(HEADS):
                pb = (h % 2) * 64
                fb = (h // 2) * 64
                r = tensor.matmul(kvp[pb:pb + 64, fb:fb + 64],
                                  expk[:, h * 64:(h + 1) * 64],
                                  v_t[:, h * 64:(h + 1) * 64],
                                  start=(t == 0), stop=(t == NT - 1))
            for c in range(8):
                r = tensor.matmul(sumdp[:, c:c + 1],
                                  expk[:, c * 128:(c + 1) * 128],
                                  ones_colb[:],
                                  start=(t == 0), stop=(t == NT - 1))
            return r
        if debug and t == 0:
            P.op("gpsimd", lambda g: g.dma_start(out=dbg["d_expk0"], in_=expk[:]),
                 waits=[(sp, sp.val)], inc=gld, inc_n=16)
            P.op("gpsimd", lambda g: g.dma_start(out=dbg["d_v0"], in_=v_t[:]),
                 waits=[(vp, vp.val)], inc=gld, inc_n=16)
        kvw = [(sp, sp.val), (vp, vp.val)]
        if debug and t == 1:
            kvw.append((gld, gld.val))
        if debug and t == NT - 1:
            P.op("gpsimd", lambda g: g.dma_start(out=dbg["d_expk63"], in_=expk[:]),
                 waits=[(sp, sp.val)], inc=gld, inc_n=16)
        if debug and t == 8:
            P.op("gpsimd", lambda g: g.dma_start(out=dbg["d_expk8"], in_=expk[:]),
                 waits=[(sp, sp.val)], inc=gld, inc_n=16)
            kvw.append((gld, gld.val))
        P.op("tensor", mk_kv, waits=kvw, inc=tp)
    t_p1 = tp.val

    P.op("vector", lambda v: v.tensor_copy(t8a[:], sumdp[:, 0:8]), waits=[(tp, t_p1)], inc=vp)
    if debug:
        P.op("vector", lambda v: v.tensor_copy(SF[:, 0:512], kvp[:]), inc=vp)
        P.op("gpsimd", lambda g: g.dma_start(out=dbg["d_kvraw"], in_=SF[:, 0:512]),
             waits=[(vp, vp.val)], inc=gld, inc_n=16)
    P.op("vector", lambda v: v.reciprocal(t8b[:], t8a[:]),
         waits=[(vp, vp.val)], inc=vp)
    P.op("vector", lambda v: v.memset(kvh[:], 0.0), waits=[(tp, t_p1)], inc=vp)
    for h in range(HEADS):
        q_ = h % 2
        j = h // 2
        P.op("vector", lambda v, q_=q_, j=j: v.tensor_single_scalar(
            kvh[q_ * 64:(q_ + 1) * 64, j * 128 + q_ * 64:j * 128 + (q_ + 1) * 64],
            kvp[q_ * 64:(q_ + 1) * 64, j * 64:(j + 1) * 64],
            t8b[q_ * 64:(q_ + 1) * 64, j:j + 1], ALU.mult),
            waits=[(vp, vp.val)] if h == 0 else (), inc=vp)
    v_kvh = vp.val
    if debug:
        P.op("gpsimd", lambda g: g.dma_start(out=dbg["d_kvh"], in_=kvh[:]),
             waits=[(vp, v_kvh)], inc=gld, inc_n=16)
        P.op("gpsimd", lambda g: g.dma_start(out=dbg["d_sumd"], in_=t8a[:]), inc=gld, inc_n=16)

    P.mark("p2")
    for c in range(8):
        P.op("sync", lambda s, c=c: s.dma_start(
            out=wqT[:, c, :], in_=wqkv_bf[0:1024, c * 128:(c + 1) * 128],
            transpose=True), waits=[(tp, t_p1), (gld, g_wqkv)], inc=tr, inc_n=16)
    for c in range(8):
        P.op("sync", lambda s, c=c: s.dma_start(
            out=woutT[:, c, :], in_=wout_bf[0:1024, c * 128:(c + 1) * 128],
            transpose=True), waits=[(gld, g_wout)], inc=tr, inc_n=16)
    y_wq = tr.val

    P.op("tensor", lambda tn: tn.matmul(BK[5][:], ones_bfr[0:1, 0:128],
                                        lng_row[0:1, 0:512], start=True, stop=True),
         waits=[(tp, t_p1), (vp, v_kvh), (sp, sp.val), (gld, g_lnrows), (tr, y_wq)], inc=tp)
    P.op("tensor", lambda tn: tn.matmul(BK[0][:], ones_bfr[0:1, 0:128],
                                        lng_row[0:1, 512:1024], start=True, stop=True), inc=tp)
    P.op("vector", lambda v: v.tensor_copy(lng_bc[0:128, 0:512], BK[5][:]),
         waits=[(tp, tp.val)], inc=vp)
    P.op("vector", lambda v: v.tensor_copy(lng_bc[0:128, 512:1024], BK[0][:]), inc=vp)
    P.op("tensor", lambda tn: tn.matmul(BK[5][:], ones_bfr[0:1, 0:128],
                                        lnb_row[0:1, 0:512], start=True, stop=True),
         waits=[(vp, vp.val)], inc=tp)
    P.op("tensor", lambda tn: tn.matmul(BK[0][:], ones_bfr[0:1, 0:128],
                                        lnb_row[0:1, 512:1024], start=True, stop=True), inc=tp)
    P.op("vector", lambda v: v.tensor_copy(lnb_bc[0:128, 0:512], BK[5][:]),
         waits=[(tp, tp.val)], inc=vp)
    P.op("vector", lambda v: v.tensor_copy(lnb_bc[0:128, 512:1024], BK[0][:]), inc=vp)
    v_lnbc = vp.val

    def ln_chain(out_ap, src_ap):
        P.op("vector", lambda v: v.bn_stats(bnbuf[:, 0:6], src_ap[:, 0:512]),
             waits=[(vp, vp.val)], inc=vp)
        P.op("vector", lambda v: v.bn_stats(bnbuf[:, 6:12], src_ap[:, 512:1024]), inc=vp)
        P.op("vector", lambda v: v.bn_aggr(mv2[:], bnbuf[:]),
             waits=[(vp, vp.val)], inc=vp)
        P.op("vector", lambda v: v.tensor_single_scalar(c1a[:], mv2[:, 1:2], EPS, ALU.add),
             waits=[(vp, vp.val)], inc=vp)
        P.op("scalar", lambda s: s.activation(c1b[:], c1a[:], AF.Sqrt),
             waits=[(vp, vp.val)], inc=sp)
        P.op("vector", lambda v: v.reciprocal(c1c[:], c1b[:]), waits=[(sp, sp.val)], inc=vp)
        P.op("vector", lambda v: v.tensor_scalar(c1d[:], mv2[:, 0:1], c1c[:], -1.0,
                                                 ALU.mult, ALU.mult),
             waits=[(vp, vp.val)], inc=vp)
        P.op("vector", lambda v: v.tensor_scalar(out_ap, src_ap, c1c[:], c1d[:],
                                                 ALU.mult, ALU.add),
             waits=[(vp, vp.val)], inc=vp)

    xt_free = {}
    y1st = {}
    gh_free = {}
    pab_free = {}
    ln_done = {}
    sq_done = {}
    vgh_done = {}
    tattn = {}
    aev = {}
    y1v = {}
    yxt = {}

    def attn_stage(t):
        g_, tl = t // 4, t % 4
        pa, pb = (BK[2], BK[3]) if tl % 2 == 0 else (BK[4], BK[5])
        P.op("sync", lambda s, t=t: s.dma_start(
            out=x_t[t % 2], in_=x[t * 128:(t + 1) * 128, :]),
            waits=[(vp, xt_free.get(t - 2, 0))], inc=yld, inc_n=16)
        yxt[t] = yld.val
        def mk_attn(tensor, pa=pa, pb=pb, tl=tl):
            r = None
            for c in range(8):
                dst = pa if c < 4 else pb
                r = tensor.matmul(dst[:, (c % 4) * 128:(c % 4 + 1) * 128],
                                  q_cm[:, c, tl * 128:(tl + 1) * 128],
                                  kvh[:, c * 128:(c + 1) * 128],
                                  start=True, stop=True)
            return r
        P.op("tensor", mk_attn,
             waits=[(sp, sq_done[g_]), (vp, v_kvh), (vp, pab_free.get(t - 2, 0))], inc=tp)
        tattn[t] = tp.val
        buf = attn2[t % 2]
        P.op("scalar", lambda s, pa=pa, buf=buf: s.copy(buf[:, 0:512], pa[:]),
             waits=[(tp, tattn[t]), (vp, y1v.get(t - 2, 0))], inc=sp)
        P.op("scalar", lambda s, pb=pb, buf=buf: s.copy(buf[:, 512:1024], pb[:]), inc=sp)
        aev[t] = sp.val

    for g in range(NG):
        halfB = g >= 8
        if g == 8:
            for c in range(8):
                P.op("sync", lambda s, c=c: s.dma_start(
                    out=xcmA[:, c, :],
                    in_=x_bf[4096:8192, c * 128:(c + 1) * 128],
                    transpose=True), waits=[(tp, tp.val)], inc=tr, inc_n=16)
        y_xg = tr.val
        goff = (g % 8) * 512
        sq0 = sp.val
        for oc in range(8):
            def mk_q(tensor, oc=oc, goff=goff):
                for c in range(8):
                    tensor.matmul(BK[oc % 2][:], wqT[:, c, oc * 128:(oc + 1) * 128],
                                  xcmA[:, c, goff:goff + 512],
                                  start=(c == 0), stop=False)
                return tensor.matmul(BK[oc % 2][:], bqkv_bf[0:1, oc * 128:(oc + 1) * 128],
                                     ones_bfr[0:1, 0:512], start=False, stop=True)
            w = [(tr, y_xg if halfB else y_wq), (vp, v_lnbc)]
            w.append((sp, sq0 + oc - 1) if oc >= 2 else (sp, sq0))
            P.op("tensor", mk_q, waits=w, inc=tp)
            P.op("scalar", lambda s, oc=oc: s.activation(
                q_cm[:, oc, :], BK[oc % 2][:], AF.Silu),
                waits=[(tp, tp.val)], inc=sp)
        sq_done[g] = sp.val
        def mk_gh(tensor, goff=goff):
            for c in range(8):
                tensor.matmul(BK[6][0:64, :], wg1T[:, c * 64:(c + 1) * 64],
                              xcmA[:, c, goff:goff + 512],
                              start=(c == 0), stop=False)
            return tensor.matmul(BK[6][0:64, :], bg1_bf[0:1, :],
                                 ones_bfr[0:1, 0:512], start=False, stop=True)
        P.op("tensor", mk_gh, waits=[(vp, v_kvh), (vp, gh_free.get(g - 1, 0))], inc=tp)
        P.op("vector", lambda v: v.tensor_copy(gh_cm, BK[6][0:64, :]),
             waits=[(tp, tp.val)], inc=vp)
        v_gh = vp.val
        gh_free[g] = vp.val
        s_q = sp.val

        if debug and g == 0:
            P.op("gpsimd", lambda g: g.dma_start(
                out=dbg["d_q"], in_=q_cm[:]),
                waits=[(sp, s_q)], inc=gld, inc_n=16)

        attn_stage(4 * g)

        def ln1_prep(t):
            buf = attn2[t % 2]
            P.op("vector", lambda v, t=t: v.engine_nop(), waits=[(sp, aev[t])], inc=vp)
            ln_chain(ln_t[:, :], buf[:, :])
            P.op("vector", lambda v: v.tensor_mul(ln_t2[:], ln_t[:], lng_bc[:]),
                 waits=[(vp, vp.val)], inc=vp)
            P.op("vector", lambda v: v.tensor_add(ln_t2[:], ln_t2[:], lnb_bc[:]),
                 waits=[(vp, vp.val)], inc=vp)
            ln_done[t] = vp.val

        ln1_prep(4 * g)

        for tl in range(4):
            t = g * 4 + tl
            pa, pb = (BK[2], BK[3]) if tl % 2 == 0 else (BK[4], BK[5])
            buf = attn2[t % 2]
            v_ln = ln_done[t]
            def mk_gate(tensor, pa=pa, pb=pb, tl=tl):
                r = None
                for hh, dst in ((0, pa), (1, pb)):
                    tensor.matmul(dst[:], gh_cm[:, tl * 128:(tl + 1) * 128],
                                  wg2T[0:64, hh * 512:(hh + 1) * 512],
                                  start=True, stop=False)
                    r = tensor.matmul(dst[:], ones_bfr[0:1, 0:128],
                                      bg2_bf[0:1, hh * 512:(hh + 1) * 512],
                                      start=False, stop=True)
                return r
            P.op("tensor", mk_gate, waits=[(sp, aev[t]), (vp, v_gh)], inc=tp)
            t_gate = tp.val
            if tl < 3:
                attn_stage(t + 1)
            for hh, src in ((0, pa), (1, pb)):
                P.op("scalar", lambda s, hh=hh, src=src: s.activation(
                    gate_t[:, hh * 512:(hh + 1) * 512], src[:], AF.Sigmoid),
                    waits=[(tp, t_gate)], inc=sp)
            s_gate = sp.val
            P.op("vector", lambda v: v.tensor_mul(g_t[:], ln_t2[:], gate_t[:]),
                 waits=[(sp, s_gate), (vp, vp.val)], inc=vp)
            v_gt = vp.val
            if debug and t == 0:
                P.op("gpsimd", lambda g, buf=buf: g.dma_start(out=dbg["d_attn"], in_=buf[:, :]),
                     waits=[(vp, v_ln)], inc=gld, inc_n=16)
                P.op("gpsimd", lambda g: g.dma_start(out=dbg["d_gate"], in_=gate_t[:]),
                     waits=[(vp, v_gt)], inc=gld, inc_n=16)
            def mk_tr(tensor):
                r = None
                for c in range(8):
                    r = tensor.transpose(PTR[:, c * 128:(c + 1) * 128],
                                         g_t[:, c * 128:(c + 1) * 128], ident_bf[:])
                return r
            P.op("tensor", mk_tr, waits=[(vp, v_gt)], inc=tp)
            P.op("scalar", lambda s: s.copy(
                g_cm[:].rearrange("p a b -> p (a b)"), PTR[:]),
                waits=[(tp, tp.val)], inc=sp)
            s_gcm = sp.val
            def mk_out(tensor, pa=pa, pb=pb):
                r = None
                for hh, dst in ((0, pa), (1, pb)):
                    for c in range(8):
                        tensor.matmul(dst[:], g_cm[:, c, :],
                                      woutT[:, c, hh * 512:(hh + 1) * 512],
                                      start=(c == 0), stop=False)
                    r = tensor.matmul(dst[:], ones_bfr[0:1, 0:128],
                                      bout_bf[0:1, hh * 512:(hh + 1) * 512],
                                      start=False, stop=True)
                return r
            P.op("tensor", mk_out, waits=[(sp, s_gcm)], inc=tp)
            t_out = tp.val
            if tl < 3:
                ln1_prep(t + 1)
            P.op("vector", lambda v, pa=pa, t=t, buf=buf: v.tensor_add(
                buf[:, 0:512], x_t[t % 2][:, 0:512], pa[:]),
                waits=[(tp, t_out), (yld, yxt[t])], inc=vp)
            P.op("vector", lambda v, pb=pb, t=t, buf=buf: v.tensor_add(
                buf[:, 512:1024], x_t[t % 2][:, 512:1024], pb[:]), inc=vp)
            xt_free[t] = vp.val
            pab_free[t] = vp.val
            ln_chain(y1_t[:, :], buf[:, :])
            v_y1 = vp.val
            y1v[t] = vp.val
            P.op("scalar", lambda s, t=t: s.dma_start(
                out=y1_dram[t * 128:(t + 1) * 128, :], in_=y1_t[:]),
                waits=[(vp, v_y1), (scst, y1st.get(t - 1, 0))], inc=scst, inc_n=16)
            y1st[t] = scst.val
            if debug and t == 0:
                P.op("gpsimd", lambda g: g.dma_start(out=dbg["d_xt"], in_=x_t[0]),
                     waits=[(vp, v_y1)], inc=gld, inc_n=16)
                P.op("gpsimd", lambda g: g.dma_start(out=dbg["d_lnbc"], in_=lng_bc), inc=gld, inc_n=16)
                P.op("gpsimd", lambda g: g.dma_start(out=dbg["d_lnt"], in_=ln_t), inc=gld, inc_n=16)
                P.op("gpsimd", lambda g, buf=buf: g.dma_start(out=dbg["d_res"], in_=buf[:, :]), inc=gld, inc_n=16)
                P.op("gpsimd", lambda g: g.dma_start(out=dbg["d_lnt2"], in_=ln_t2[:]), inc=gld, inc_n=16)
                P.op("gpsimd", lambda g: g.dma_start(
                    out=dbg["d_gcm"], in_=g_cm[:].rearrange("p a b -> p (a b)")), inc=gld, inc_n=16)
                P.op("vector", lambda v: v.engine_nop(), waits=[(gld, gld.val)], inc=vp)
    t_p2 = tp.val
    v_p2 = vp.val
    s_p2 = sp.val
    sc_y1 = scst.val

    P.mark("p3")
    for half, dst in [(0, ycmA), (1, ycmB)]:
        for c in range(8):
            P.op("sync", lambda s, dst=dst, c=c, half=half: s.dma_start(
                out=dst[:, c, :],
                in_=y1_dram[half * 4096:(half + 1) * 4096, c * 128:(c + 1) * 128],
                transpose=True),
                waits=[(tp, t_p2), (scst, sc_y1), (vp, v_p2), (sp, s_p2)],
                inc=tr, inc_n=16)
    y_ycm = tr.val
    if debug:
        P.op("gpsimd", lambda g: g.dma_start(
            out=dbg["d_y1r"], in_=y1_dram[0:128, :]), waits=[(scst, sc_y1)], inc=gld, inc_n=16)
        P.op("gpsimd", lambda g: g.dma_start(
            out=dbg["d_ycm"], in_=ycmA[:, 0, 0:1024]), waits=[(tr, y_ycm)], inc=gld, inc_n=16)
        P.op("gpsimd", lambda g: g.dma_start(
            out=dbg["d_wgr"], in_=wgr[:]), inc=gld, inc_n=16)

    lg_free = {}
    for t in range(NT):
        ysrc = ycmA if t < 32 else ycmB
        toff = (t % 32) * 128
        pl = BK[2] if t % 2 == 0 else BK[3]
        def mk_lg(tensor, ysrc=ysrc, toff=toff, pl=pl):
            r = None
            for c in range(8):
                r = tensor.matmul(pl[:, 0:8], ysrc[:, c, toff:toff + 128],
                                  wgr[:, c * 8:(c + 1) * 8],
                                  start=(c == 0), stop=(c == 7))
            return r
        P.op("tensor", mk_lg,
             waits=[(tr, y_ycm), (vp, lg_free.get(t - 2, v_p2))], inc=tp)
        t_lg = tp.val
        if t == 31:
            t_lgA = tp.val
        P.op("vector", lambda v, pl=pl: v.tensor_copy(t8a[:], pl[:, 0:8]),
             waits=[(tp, t_lg)], inc=vp)
        lg_free[t] = vp.val
        if debug and t == 0:
            P.op("gpsimd", lambda g: g.dma_start(out=dbg["d_lg"], in_=t8a[:]),
                 waits=[(vp, vp.val)], inc=gld, inc_n=16)
        P.op("vector", lambda v: v.reduce_max(c1a[:], t8a[:], axis=AX.X),
             waits=[(vp, vp.val)], inc=vp)
        P.op("vector", lambda v: v.reduce_max(c1b[:], t8a[:], axis=AX.X, negate=True), inc=vp)
        P.op("scalar", lambda s: s.activation(t8b[:], t8a[:], AF.Exp, bias=c1b[:]),
             waits=[(vp, vp.val)], inc=sp)
        P.op("vector", lambda v: v.reduce_sum(c1c[:], t8b[:], axis=AX.X),
             waits=[(sp, sp.val)], inc=vp)
        P.op("vector", lambda v, t=t: v.reciprocal(gcol[:, t:t + 1], c1c[:]),
             waits=[(vp, vp.val)], inc=vp)
        P.op("vector", lambda v, t=t: v.tensor_scalar(
            mask_b[:, t * 8:(t + 1) * 8], t8a[:], c1a[:], None, ALU.is_equal),
            waits=[(vp, vp.val)], inc=vp)
        P.op("vector", lambda v, t=t: v.tensor_mul(
            t8c_b[:], mask_b[:, t * 8:(t + 1) * 8], iota8[:]),
            waits=[(vp, vp.val)], inc=vp)
        P.op("vector", lambda v, t=t: v.reduce_sum(ecol[:, t:t + 1], t8c_b[:], axis=AX.X),
             waits=[(vp, vp.val)], inc=vp)
        v_mk = vp.val
        P.op("tensor", lambda tensor, t=t: tensor.matmul(
            BK[6][0:8, t:t + 1], mask_b[:, t * 8:(t + 1) * 8], ones_colb[:],
            start=True, stop=True), waits=[(vp, v_mk)], inc=tp)
    t_tot = tp.val
    P.op("vector", lambda v: v.tensor_copy(tots[:, :], BK[6][0:8, 0:NT]),
         waits=[(tp, t_tot)], inc=vp)
    P.op("vector", lambda v: v.tensor_tensor_scan(
        offsx[:, :], tots[:, :], tots[:, :], 0.0, ALU.add, ALU.bypass),
        waits=[(vp, vp.val)], inc=vp)
    P.op("vector", lambda v: v.tensor_sub(offsx[:, :], offsx[:, :], tots[:, :]),
         waits=[(vp, vp.val)], inc=vp)
    v_offs = vp.val
    for e_ in range(8):
        P.op("gpsimd", lambda g, e_=e_: g.dma_start(
            out=offs_flat[0:1, :].rearrange("q (g r) -> q g r", r=8)[:, :, e_],
            in_=offsx[e_:e_ + 1, :], allow_slow_non_contiguous=True),
            waits=[(vp, v_offs), (gld, g_zero)], inc=gld, inc_n=16)
    g_offs = gld.val

    incl_all = SF[:, 0:512]
    rkbuf = SF[:, 512:1024]
    r1_ = SF[:, 0:64]
    kept_ = SF[:, 64:128]
    s1_ = SF[:, 128:192]
    tr_ = SF[:, 192:256]
    cum_free = {}
    for t in range(NT):
        pc = BK[2] if t % 2 == 0 else BK[4]
        po = BK[3] if t % 2 == 0 else BK[5]
        def mk_cum(tensor, t=t, pc=pc, po=po):
            tensor.matmul(pc[:, 0:8], ut_bf[:], mask_b[:, t * 8:(t + 1) * 8],
                          start=True, stop=True)
            return tensor.matmul(po[:, 0:8], ones_f32r[0:1, 0:128],
                                 offs_flat[0:1, t * 8:(t + 1) * 8],
                                 start=True, stop=True)
        P.op("tensor", mk_cum,
             waits=[(gld, g_offs), (vp, cum_free.get(t - 2, v_offs))], inc=tp)
        t_cum = tp.val
        P.op("vector", lambda v, pc=pc: v.tensor_copy(t8a[:], pc[:, 0:8]),
             waits=[(tp, t_cum), (vp, vp.val)], inc=vp)
        P.op("vector", lambda v, po=po, t=t: v.tensor_add(
            incl_all[:, t * 8:(t + 1) * 8], t8a[:], po[0:128, 0:8]),
            waits=[(vp, vp.val)], inc=vp)
        cum_free[t] = vp.val
    P.op("vector", lambda v: v.tensor_copy(rkbuf[:], mask_b[:, :]),
         waits=[(vp, vp.val)], inc=vp)
    P.op("vector", lambda v: v.tensor_mul(rkbuf[:], rkbuf[:], incl_all[:]),
         waits=[(vp, vp.val)], inc=vp)
    P.op("vector", lambda v: v.reduce_sum(
        r1_[:].rearrange("p (t e) -> p t e", e=1),
        rkbuf[:].rearrange("p (t e) -> p t e", e=8), axis=AX.X),
        waits=[(vp, vp.val)], inc=vp)
    P.op("vector", lambda v: v.tensor_single_scalar(kept_[:], r1_[:], float(CAP), ALU.is_le),
         waits=[(vp, vp.val)], inc=vp)
    P.op("vector", lambda v: v.tensor_mul(kg128[:, :], gcol[:, :], kept_[:]),
         waits=[(vp, vp.val)], inc=vp)
    P.op("vector", lambda v: v.tensor_single_scalar(s1_[:], ecol[:, :], float(CAP), ALU.mult),
         inc=vp)
    P.op("vector", lambda v: v.tensor_add(s1_[:], s1_[:], r1_[:]),
         waits=[(vp, vp.val)], inc=vp)
    P.op("vector", lambda v: v.tensor_scalar(
        tr_[:], kept_[:], -float(TRASH + 1), float(TRASH), ALU.mult, ALU.add), inc=vp)
    P.op("vector", lambda v: v.tensor_mul(s1_[:], s1_[:], kept_[:]),
         waits=[(vp, vp.val)], inc=vp)
    P.op("vector", lambda v: v.tensor_add(s1_[:], s1_[:], tr_[:]),
         waits=[(vp, vp.val)], inc=vp)
    P.op("vector", lambda v: v.tensor_copy(slot128[:, :], s1_[:]),
         waits=[(vp, vp.val)], inc=vp)
    v_slot = vp.val
    if debug:
        P.op("vector", lambda v: v.tensor_copy(SF[:, 0:64], slot128[:, :]),
             waits=[(vp, v_slot)], inc=vp)
        P.op("scalar", lambda s: s.dma_start(out=dbg["d_slot"], in_=SF[:, 0:64]),
             waits=[(vp, vp.val)], inc=scst, inc_n=16)
        P.op("scalar", lambda s: s.dma_start(out=dbg["d_kg"], in_=kg128[:]), inc=scst, inc_n=16)

    for r in range(8):
        P.op("gpsimd", lambda g, r=r: g.dma_start(
            out=idx16[0:16, :].rearrange("q (g r) -> q g r", r=8)[:, :, r],
            in_=slot128[r * 16:(r + 1) * 16, :],
            allow_slow_non_contiguous=True),
            waits=[(vp, v_slot)], inc=gld, inc_n=16)
    g_idxa = gld.val
    for k in range(1, 8):
        P.op("gpsimd", lambda g, k=k: g.dma_start(
            out=idx16[k * 16:(k + 1) * 16, :], in_=idx16[0:16, :]),
            waits=[(gld, g_idxa)], inc=gld, inc_n=16)
    g_idx = gld.val
    if debug:
        P.op("scalar", lambda s: s.dma_start(out=dbg["d_idx"], in_=idx16[:]),
             waits=[(gld, g_idx)], inc=scst, inc_n=16)

    for half, dst in [(0, y1tmA), (1, y1tmB)]:
        for q in range(4):
            P.op("sync", lambda s, dst=dst, half=half, q=q: s.dma_start(
                out=dst[:, q * 8:(q + 1) * 8, :],
                in_=y1_dram[half * 4096 + q * 1024: half * 4096 + (q + 1) * 1024, :]
                    .rearrange("(a p) d -> p a d", p=128)),
                waits=[(tp, t_lgA if half == 0 else t_tot)], inc=yld, inc_n=16)
    y_y1tm = yld.val

    P.op("gpsimd", lambda g: g.dma_scatter_add(
        ei_dram, y1tmA[:], idx16[:, 0:256], 4096, 4096, D),
        waits=[(gld, g_idx), (yld, y_y1tm)], inc=gsc, inc_n=16)
    g_scatA = gsc.val
    P.op("gpsimd", lambda g: g.dma_scatter_add(
        ei_dram, y1tmB[:], idx16[:, 256:512], 4096, 4096, D), inc=gsc, inc_n=16)
    g_scat = gsc.val

    P.mark("p4")
    t_w1_done = {}
    t_w2_done = {}
    w1ck = {}
    w2ck = {}
    for ex in range(NE):
        if ex == 0:
            P.op("gpsimd", lambda g: g.dma_start(
                out=w1sb[:], in_=w1[0].rearrange("(c p) h -> p c h", p=128)),
                waits=[(gsc, g_scatA)], inc=gld, inc_n=16)
            g_w1 = gld.val
            P.op("gpsimd", lambda g: g.dma_start(
                out=w2sb[:], in_=w2[0].rearrange("(c p) d -> p c d", p=128)),
                waits=[(gsc, g_scat)], inc=gld, inc_n=16)
            g_w2 = gld.val
        eiw = [(gsc, g_scat)]
        if ex > 0:
            eiw.append((tp, t_w1_done[ex - 1]))
        for c in range(8):
            P.op("sync", lambda s, ex=ex, c=c: s.dma_start(
                out=ei_cm[:, c, :],
                in_=ei_dram[ex * 1024:(ex + 1) * 1024, c * 128:(c + 1) * 128],
                transpose=True), waits=eiw if c == 0 else (), inc=tr, inc_n=16)
        y_ei = tr.val

        for sc in range(4):
            eob = eo_sb[:, (sc % 2) * 2:(sc % 2) * 2 + 2, :]
            s40 = sp.val
            gelu_done = {}
            def mk_w2(tensor, hc, sc=sc):
                r = None
                for sub in range(2):
                    for dh in range(2):
                        r = tensor.matmul(BK[3 + sub * 2 + dh][:],
                                          h_ring[:, hc % 4, sub * 128:(sub + 1) * 128],
                                          w2sb[:, hc, dh * 512:(dh + 1) * 512],
                                          start=(hc == 0), stop=(hc == 31))
                return r
            for hc in range(32):
                bw = BK[hc % 3]
                def mk_w1(tensor, hc=hc, sc=sc, bw=bw):
                    r = None
                    for c in range(8):
                        r = tensor.matmul(bw[:, 0:256], w1sb[:, c, hc * 128:(hc + 1) * 128],
                                          ei_cm[:, c, sc * 256:(sc + 1) * 256],
                                          start=(c == 0), stop=(c == 7))
                    return r
                if ex == 0:
                    w = [(gld, g_w1), (tr, y_ei)]
                else:
                    w = [(gld, w1ck[(ex, hc)]), (tr, y_ei)]
                if hc >= 3:
                    w.append((sp, s40 + hc - 2))
                if hc == 0 and sc > 0:
                    w.append((vp, vp.val))
                P.op("tensor", mk_w1, waits=w, inc=tp)
                t_w1hc = tp.val
                if sc == 3 and ex < NE - 1:
                    P.op("gpsimd", lambda g, ex=ex, hc=hc: g.dma_start(
                        out=w1sb[:, :, hc * 128:(hc + 1) * 128],
                        in_=w1[ex + 1].rearrange("(c p) h -> p c h", p=128)[:, :, hc * 128:(hc + 1) * 128]),
                        waits=[(tp, t_w1hc)], inc=gld, inc_n=16)
                    w1ck[(ex + 1, hc)] = gld.val
                P.op("scalar", lambda s, hc=hc, bw=bw: s.activation(
                    h_ring[:, hc % 4, :], bw[:, 0:256], AF.Gelu_apprx_tanh),
                    waits=[(tp, t_w1hc)], inc=sp)
                gelu_done[hc] = sp.val
                if hc >= 1:
                    w2w = [(sp, gelu_done[hc - 1])]
                    w2w.append((gld, g_w2) if ex == 0 else (gld, w2ck[(ex, hc - 1)]))
                    P.op("tensor", lambda tensor, hc=hc: mk_w2(tensor, hc - 1),
                         waits=w2w, inc=tp)
                    if sc == 3 and ex < NE - 1:
                        P.op("gpsimd", lambda g, ex=ex, hc=hc: g.dma_start(
                            out=w2sb[:, hc - 1, :],
                            in_=w2[ex + 1].rearrange("(c p) d -> p c d", p=128)[:, hc - 1, :]),
                            waits=[(tp, tp.val)], inc=gld, inc_n=16)
                        w2ck[(ex + 1, hc - 1)] = gld.val
            P.op("tensor", lambda tensor: mk_w2(tensor, 31),
                 waits=[(sp, gelu_done[31])] +
                       ([(gld, g_w2)] if ex == 0 else [(gld, w2ck[(ex, 31)])]), inc=tp)
            if sc == 3 and ex < NE - 1:
                P.op("gpsimd", lambda g, ex=ex: g.dma_start(
                    out=w2sb[:, 31, :],
                    in_=w2[ex + 1].rearrange("(c p) d -> p c d", p=128)[:, 31, :]),
                    waits=[(tp, tp.val)], inc=gld, inc_n=16)
                w2ck[(ex + 1, 31)] = gld.val
            t_w2sc = tp.val
            if sc == 3:
                t_w1_done[ex] = tp.val
                t_w2_done[ex] = tp.val
            for sub in range(2):
                for dh in range(2):
                    P.op("vector", lambda v, sub=sub, dh=dh, eob=eob: v.tensor_copy(
                        eob[:, sub, dh * 512:(dh + 1) * 512], BK[3 + sub * 2 + dh][:]),
                        waits=[(tp, t_w2sc)], inc=vp)
            v_eo = vp.val
            P.op("scalar", lambda s, ex=ex, sc=sc, eob=eob: s.dma_start(
                out=eo_dram[ex * 1024 + sc * 256: ex * 1024 + (sc + 1) * 256, :]
                    .rearrange("(a p) d -> p a d", p=128),
                in_=eob), waits=[(vp, v_eo)], inc=scst, inc_n=16)
    sc_eo = scst.val

    P.mark("p5")
    ost = {}
    for ch in range(8):
        gw = [(scst, sc_eo), (gld, g_idx)]
        yw = [(scst, sc_eo)]
        if ch > 0:
            gw.append((vp, vp.val))
            yw.append((vp, vp.val))
        P.op("gpsimd", lambda g, ch=ch: g.dma_gather(
            of_sb[:], eo_dram, idx16[:, ch * 64:(ch + 1) * 64], 1024, 1024, D),
            waits=gw, inc=gld, inc_n=16)
        g_of = gld.val
        P.op("sync", lambda s, ch=ch: s.dma_start(
            out=y1c[:], in_=y1_dram[ch * 1024:(ch + 1) * 1024, :]
                .rearrange("(a p) d -> p a d", p=128)),
            waits=yw, inc=yld, inc_n=16)
        y_y1c = yld.val
        for tl in range(8):
            t = ch * 8 + tl
            ob = outstage[tl % 2]
            P.op("vector", lambda v, tl=tl, t=t: v.scalar_tensor_tensor(
                res_f[:, :], of_sb[:, tl, :], kg128[:, t:t + 1], y1c[:, tl, :],
                ALU.mult, ALU.add),
                waits=[(gld, g_of), (yld, y_y1c), (scst, ost.get(t - 2, 0))], inc=vp)
            ln_chain(ob[:, :], res_f[:, :])
            v_out = vp.val
            P.op("scalar", lambda s, t=t, ob=ob: s.dma_start(
                out=y[t * 128:(t + 1) * 128, :], in_=ob[:, :]),
                waits=[(vp, v_out)], inc=scst, inc_n=16)
            ost[t] = scst.val
    sc_out = scst.val

    P.op("sync", lambda s: s.wait_ge(scst.h, sc_out))
    P.op("gpsimd", lambda g: g.wait_ge(scst.h, sc_out))

    if isinstance(max_items, str):
        max_items = P.marks[max_items]
    import os
    if max_items is None and os.environ.get("KMAXI"):
        mi = os.environ["KMAXI"]
        max_items = P.marks[mi] if mi in P.marks else int(mi)
    with nc.Block() as block:
        @block.sync
        def _(eng):
            P.emit("sync", eng, max_items)

        @block.gpsimd
        def _(eng):
            P.emit("gpsimd", eng, max_items)

        @block.tensor
        def _(eng):
            P.emit("tensor", eng, max_items)

        @block.vector
        def _(eng):
            P.emit("vector", eng, max_items)

        @block.scalar
        def _(eng):
            P.emit("scalar", eng, max_items)
    import sys
    print(f"[kernel] program items={len(P.items)} marks={P.marks} max_items={max_items}",
          file=sys.stderr)

    st.close()
    library_overlay.lower_extended_insts(nc)
    return nc


_NC_CACHE = {}


def _get_nc(debug=False):
    key = bool(debug)
    if key not in _NC_CACHE:
        _NC_CACHE[key] = build_nc(debug=debug)
    return _NC_CACHE[key]


def kernel(L0_attn_x, L0_attn_w_qkv, L0_attn_b_qkv, L0_attn_w_g1, L0_attn_b_g1,
           L0_attn_w_g2, L0_attn_b_g2, L0_attn_w_out, L0_attn_b_out,
           L0_attn_ln_g, L0_attn_ln_b, L0_ffn_Wg, L0_ffn_W1, L0_ffn_W2):
    nc = _get_nc(debug=False)
    common = {
        "w_qkv": np.ascontiguousarray(L0_attn_w_qkv, dtype=np.float32),
        "b_qkv": np.ascontiguousarray(L0_attn_b_qkv, dtype=np.float32).reshape(1, -1),
        "w_g1": np.ascontiguousarray(L0_attn_w_g1, dtype=np.float32),
        "b_g1": np.ascontiguousarray(L0_attn_b_g1, dtype=np.float32).reshape(1, -1),
        "w_g2": np.ascontiguousarray(L0_attn_w_g2, dtype=np.float32),
        "b_g2": np.ascontiguousarray(L0_attn_b_g2, dtype=np.float32).reshape(1, -1),
        "w_out": np.ascontiguousarray(L0_attn_w_out, dtype=np.float32),
        "b_out": np.ascontiguousarray(L0_attn_b_out, dtype=np.float32).reshape(1, -1),
        "ln_g": np.ascontiguousarray(L0_attn_ln_g, dtype=np.float32).reshape(1, -1),
        "ln_b": np.ascontiguousarray(L0_attn_ln_b, dtype=np.float32).reshape(1, -1),
        "wg": np.ascontiguousarray(L0_ffn_Wg, dtype=np.float32),
        "w1": np.ascontiguousarray(L0_ffn_W1, dtype=np.float32),
        "w2": np.ascontiguousarray(L0_ffn_W2, dtype=np.float32),
    }
    xs = np.ascontiguousarray(L0_attn_x, dtype=np.float32)
    in_maps = [dict(common, x=np.ascontiguousarray(xs[b])) for b in range(8)]
    res = run_bass_kernel_spmd(nc, in_maps, core_ids=list(range(8)))
    out = np.stack([np.asarray(res.results[b]["y"]) for b in range(8)], axis=0)
    return out.astype(np.float32)
